# version 07
# baseline (speedup 1.0000x reference)
import numpy as np
import ml_dtypes
import concourse.bass as bass
import concourse.mybir as mybir
from concourse.bass_utils import run_bass_kernel_spmd

F32 = mybir.dt.float32
BF16 = mybir.dt.bfloat16
U8 = mybir.dt.uint8
ACT = mybir.ActivationFunctionType
ALU = mybir.AluOpType
AX = mybir.AxisListType

NCORES = 8
T = 1024
D = 2048
DC = 16
NE = 32
TWO_PI = 6.283185307179586
MAGIC = 8388608.0

def _marks(names):
    return {n: i + 1 for i, n in enumerate(names)}

V = _marks(
    [f"stat{i}" for i in range(8)] + ["var", "mr16", "mrsb", "mbrb"]
    + [f"xn{k}" for k in range(16)]
    + ["lgT", "bu_re0", "bu_re1", "ltok", "ssum8", "den", "comb", "combT",
       "bu_im0", "bu_im1", "tables", "f", "hfin", "cneg"]
)
S = _marks(
    ["aexp", "gamma", "r8"] + [f"cast{k}" for k in range(16)]
    + ["gexp", "inv", "dinv", "sins", "hc0", "hc1", "hc2", "hc3"]
)
T1 = _marks(
    ["mrT", "mbc", "rbc", "lg"]
    + ["bu_re0", "bu_re1", "bu_im0", "bu_im1"]
    + [f"gt{t}" for t in range(8)]
    + [f"ct{t}" for t in range(8)]
    + [f"y{d}" for d in range(16)]
)


def W(eng, sem, n):
    if n > 0:
        eng.wait_ge(sem, n)


def build_nc(ne=NE, stage=None):
    nc = bass.Bass("TRN2", target_bir_lowering=False, detect_race_conditions=False)

    xT = nc.dram_tensor("xT", [D, T], F32, kind="ExternalInput")
    xtok = nc.dram_tensor("xtok", [T, D], F32, kind="ExternalInput")
    nu_t = nc.dram_tensor("nu_t", [128, 2], F32, kind="ExternalInput")
    theta_t = nc.dram_tensor("theta_t", [128, 2], F32, kind="ExternalInput")
    d_t = nc.dram_tensor("d_t", [128, 16], F32, kind="ExternalInput")
    selc_t = nc.dram_tensor("selc_t", [128, 64], F32, kind="ExternalInput")
    bre_t = nc.dram_tensor("bre_t", [D, 256], F32, kind="ExternalInput")
    bim_t = nc.dram_tensor("bim_t", [D, 256], F32, kind="ExternalInput")
    cre_t = nc.dram_tensor("cre_t", [256, D], F32, kind="ExternalInput")
    cim_t = nc.dram_tensor("cim_t", [256, D], F32, kind="ExternalInput")
    gw_t = nc.dram_tensor("gw_t", [D, NE], F32, kind="ExternalInput")
    gu_w = nc.dram_tensor("gu_w", [NE, 6, D, 256], F32, kind="ExternalInput")
    dn_w = nc.dram_tensor("dn_w", [NE, 4, 768, 512], F32, kind="ExternalInput")
    idm_bf_d = nc.dram_tensor("idm_bf", [128, 128], BF16, kind="ExternalInput")
    idm_f32_d = nc.dram_tensor("idm_f32", [128, 128], F32, kind="ExternalInput")
    ones_bf_d = nc.dram_tensor("ones_bf", [1, 128], BF16, kind="ExternalInput")
    ones_f32_d = nc.dram_tensor("ones_f32", [1, 128], F32, kind="ExternalInput")
    consts_d = nc.dram_tensor("consts_d", [128, 2], F32, kind="ExternalInput")
    oT = nc.dram_tensor("oT", [D, T], F32, kind="ExternalOutput")
    dbg_mb = nc.dram_tensor("dbg_mb", [128, T], F32, kind="ExternalOutput")
    dbg_rb = nc.dram_tensor("dbg_rb", [128, T], F32, kind="ExternalOutput")
    dbg_mq = nc.dram_tensor("dbg_mq", [128, 40], F32, kind="ExternalOutput")
    dbg_sb = nc.dram_tensor("dbg_sb", [16, 128], F32, kind="ExternalOutput")
    cc_in = nc.dram_tensor("cc_in", [128, 4], F32, kind="Internal")
    cc_out = nc.dram_tensor("cc_out", [128 * NCORES, 4], F32, kind="Internal")

    from contextlib import ExitStack
    ctx = ExitStack()
    with ctx:
        E = ctx.enter_context
        out_acc = E(nc.sbuf_tensor("out_acc", [128, DC, T], F32))
        xn_bf = E(nc.sbuf_tensor("xn_bf", [128, DC, T], BF16))
        gu_b = [E(nc.sbuf_tensor(f"gu_b{i}", [128, 16, 256], BF16)) for i in range(2)]
        dn_b = [E(nc.sbuf_tensor(f"dn_b{i}", [128, 6, 512], BF16)) for i in range(2)]
        gw_bf = E(nc.sbuf_tensor("gw_bf", [128, 16, NE], BF16))
        idm_bf = E(nc.sbuf_tensor("idm_bf_s", [128, 128], BF16))
        idm_f32 = E(nc.sbuf_tensor("idm_f32_s", [128, 128], F32))
        ones_bf = E(nc.sbuf_tensor("ones_bf_s", [1, 128], BF16))
        ones_f32 = E(nc.sbuf_tensor("ones_f32_s", [1, 128], F32))
        nu2 = E(nc.sbuf_tensor("nu2", [128, 2], F32))
        th2 = E(nc.sbuf_tensor("th2", [128, 2], F32))
        d_col = E(nc.sbuf_tensor("d_col", [128, 16], F32))
        selc = E(nc.sbuf_tensor("selc", [128, 64], F32))
        consts = E(nc.sbuf_tensor("consts", [128, 2], F32))
        aparm = E(nc.sbuf_tensor("aparm", [128, 24], F32))
        apow_re = E(nc.sbuf_tensor("apow_re", [128, 11, 2], F32))
        apow_im = E(nc.sbuf_tensor("apow_im", [128, 11, 2], F32))
        napow_im = E(nc.sbuf_tensor("napow_im", [128, 11, 2], F32))
        qp_re = E(nc.sbuf_tensor("qp_re", [128, 8, 2], F32))
        qp_im = E(nc.sbuf_tensor("qp_im", [128, 8, 2], F32))
        f_sb = E(nc.sbuf_tensor("f_sb", [128, 4], F32))
        f_all = E(nc.sbuf_tensor("f_all", [128, 8, 4], F32))
        gacc = E(nc.sbuf_tensor("gacc", [128, 8, 4], F32))
        gtmp = E(nc.sbuf_tensor("gtmp", [128, 7, 4, 7], F32))
        qg = E(nc.sbuf_tensor("qg", [128, 8, 2, 4], F32))
        hsum = E(nc.sbuf_tensor("hsum", [128, 2, 4], F32))
        hin = E(nc.sbuf_tensor("hin", [128, 4], F32))
        ce_row = [E(nc.sbuf_tensor(f"ce_row{i}", [1, T], BF16)) for i in range(2)]
        mq = E(nc.sbuf_tensor("mq", [128, 40], F32))
        mr16 = E(nc.sbuf_tensor("mr16", [128, 16], F32))
        mr_sb = E(nc.sbuf_tensor("mr_sb", [16, 128], F32))
        gat = E(nc.sbuf_tensor("gat", [128, 8, 32], F32))
        gat2 = E(nc.sbuf_tensor("gat2", [128, 8, 32], F32))
        gat3 = E(nc.sbuf_tensor("gat3", [128, 8, 32], F32))
        tv = E(nc.sbuf_tensor("tv", [128, 8, 8], F32))
        cumb = E(nc.sbuf_tensor("cumb", [128, 8, 8], F32))
        gsml = E(nc.sbuf_tensor("gsml", [128, 48], F32))
        combT = E(nc.sbuf_tensor("combT", [32, T], BF16))
        h0_p = E(nc.sbuf_tensor("h0_p", [128, 6, T], BF16))
        gsil_p = E(nc.sbuf_tensor("gsil_p", [128, 6, T], BF16))
        ceb0_p = E(nc.sbuf_tensor("ceb0_p", [128, T], BF16))

        arena = E(nc.sbuf_tensor("arena", [128, 40960], U8))

        def afr(byte0, nbytes, dt=F32):
            return arena[:, byte0:byte0 + nbytes].bitcast(dt)

        tokstage = [afr(0, 8192), afr(8192, 8192)]
        sq_tmp = afr(16384, 8192)
        xstage = [afr(0, 4096), afr(4096, 4096)]
        xn32 = [afr(8192, 4096), afr(12288, 4096)]
        m_b = afr(16384, 4096)
        r_b = afr(20480, 4096)
        bt_re = afr(0, 8192, BF16).rearrange("p (c n) -> p c n", c=16)
        bt_im = afr(8192, 8192, BF16).rearrange("p (c n) -> p c n", c=16)
        bufA_re = afr(16384, 4096)
        bufA_im = afr(20480, 4096)
        bufB_re = afr(24576, 4096)
        bufB_im = afr(28672, 4096)
        lgT_sb = afr(32768, 4096)[0:32, :]
        png_re = [afr(0, 4096), afr(8192, 4096)]
        png_im = [afr(4096, 4096), afr(12288, 4096)]
        H_bf = afr(32768, 8192, BF16).rearrange("p (b t) -> p b t", b=4)
        ctbuf = afr(0, 16384, BF16).rearrange("p (b t) -> p b t", b=4)
        h_b = [h0_p[:], afr(12288, 12288, BF16).rearrange("p (j t) -> p j t", j=6)]
        gsil = gsil_p[:]
        ce_b = [ceb0_p[:], afr(18432, 2048, BF16)]

        pA = E(nc.psum_tensor("pA", [128, 1024], F32))
        pB = E(nc.psum_tensor("pB", [128, 1024], F32))
        pC = E(nc.psum_tensor("pC", [128, 1024], F32))
        pD = E(nc.psum_tensor("pD", [128, 1024], F32))
        pg = [pA, pB]
        pd = [pC, pD]

        s_misc = E(nc.semaphore("s_misc"))
        s_gldg = E(nc.semaphore("s_gldg"))
        s_tok = [E(nc.semaphore(f"s_tok{i}")) for i in range(2)]
        s_xc = [E(nc.semaphore(f"s_xc{i}")) for i in range(2)]
        s_w = [E(nc.semaphore(f"s_w{i}")) for i in range(2)]
        s_d = [E(nc.semaphore(f"s_d{i}")) for i in range(2)]
        s_ce = [E(nc.semaphore(f"s_ce{i}")) for i in range(2)]
        s_mrow = E(nc.semaphore("s_mrow"))
        s_ccs = E(nc.semaphore("s_ccs"))
        s_fal = E(nc.semaphore("s_fal"))
        s_v1 = E(nc.semaphore("s_v1"))
        s_s1 = E(nc.semaphore("s_s1"))
        s_t1 = E(nc.semaphore("s_t1"))
        s_gud = E(nc.semaphore("s_gud"))
        s_gue = E(nc.semaphore("s_gue"))
        s_sil = E(nc.semaphore("s_sil"))
        s_dnd = E(nc.semaphore("s_dnd"))
        s_dne = E(nc.semaphore("s_dne"))
        s_yev = E(nc.semaphore("s_yev"))
        s_st = E(nc.semaphore("s_st"))

        nc.const_aps.register(F32, -1.0, consts[:, 0:1])

        blk = E(nc.Block())

        @blk.sync
        def _(sync):
            sync.dma_start(nu2[:], nu_t[:, :]).then_inc(s_misc, 16)
            sync.dma_start(th2[:], theta_t[:, :]).then_inc(s_misc, 16)
            sync.dma_start(d_col[:], d_t[:, :]).then_inc(s_misc, 16)
            sync.dma_start(selc[:], selc_t[:, :]).then_inc(s_misc, 16)
            sync.dma_start(idm_bf[:], idm_bf_d[:, :]).then_inc(s_misc, 16)
            sync.dma_start(idm_f32[:], idm_f32_d[:, :]).then_inc(s_misc, 16)
            sync.dma_start(ones_bf[:], ones_bf_d[:, :]).then_inc(s_misc, 16)
            sync.dma_start(ones_f32[:], ones_f32_d[:, :]).then_inc(s_misc, 16)
            sync.dma_start(consts[:], consts_d[:, :]).then_inc(s_misc, 16)
            for i in range(8):
                if i >= 2:
                    sync.wait_ge(s_v1, V[f"stat{i - 2}"])
                sync.dma_start(tokstage[i % 2], xtok[i * 128:(i + 1) * 128, :]).then_inc(s_tok[i % 2], 16)
            sync.wait_ge(s_v1, V["mrsb"])
            for a in range(8):
                sync.dma_start(m_b[0:1, a * 128:(a + 1) * 128], mr_sb[a:a + 1, 0:128]).then_inc(s_mrow, 16)
            for a in range(8):
                sync.dma_start(r_b[0:1, a * 128:(a + 1) * 128], mr_sb[8 + a:9 + a, 0:128]).then_inc(s_mrow, 16)
            for k in range(16):
                if k < 2:
                    sync.wait_ge(s_v1, V["mbrb"])
                else:
                    sync.wait_ge(s_v1, V[f"xn{k - 2}"])
                sync.dma_start(xstage[k % 2], xT[k * 128:(k + 1) * 128, :]).then_inc(s_xc[k % 2], 16)
            for e in range(min(2, ne)):
                sync.wait_ge(s_v1, V["combT"])
                sync.dma_start(ce_row[e % 2][:], combT[e:e + 1, :]).then_inc(s_ce[e % 2], 16)
            sync.wait_ge(s_v1, V["f"])
            sync.dma_start(cc_in[:, :], f_sb[:]).then_inc(s_ccs, 16)
            for e in range(2, ne):
                sync.wait_ge(s_gud, 13 * (e - 2) + 1)
                sync.dma_start(ce_row[e % 2][:], combT[e:e + 1, :]).then_inc(s_ce[e % 2], 16)
            sync.wait_ge(s_v1, V["xn15"])
            sync.dma_start(dbg_mb[:, :], m_b[:, :]).then_inc(s_st, 16)
            sync.dma_start(dbg_rb[:, :], r_b[:, :]).then_inc(s_st, 16)
            sync.dma_start(dbg_mq[:, :], mq[:]).then_inc(s_st, 16)
            sync.dma_start(dbg_sb[:, :], mr_sb[:]).then_inc(s_st, 16)
            if stage == "xn":
                sync.wait_ge(s_v1, V["xn15"])
            elif stage == "hfin":
                sync.wait_ge(s_v1, V["hfin"])
            elif ne > 0:
                sync.wait_ge(s_dne, 16 * ne)
            else:
                sync.wait_ge(s_yev, 16)
            sync.dma_start(oT[:, :].rearrange("(c p) t -> p c t", p=128), out_acc[:]).then_inc(s_st, 16)
            sync.wait_ge(s_st, 16 * 5)

        @blk.gpsimd
        def _(gpsimd):
            gpsimd.dma_start(gw_bf[:], gw_t[:, :].rearrange("(c p) e -> p c e", p=128)).then_inc(s_gldg, 16)
            gpsimd.wait_ge(s_v1, V["xn15"])
            gpsimd.wait_ge(s_s1, S["cast15"])
            gpsimd.dma_start(bt_re, bre_t[:, :].rearrange("(c p) n -> p c n", p=128)).then_inc(s_gldg, 16)
            gpsimd.dma_start(bt_im, bim_t[:, :].rearrange("(c p) n -> p c n", p=128)).then_inc(s_gldg, 16)
            for g6 in range(min(8, 6 * ne)):
                if g6 >= 2:
                    ep, sp = divmod(g6 - 2, 6)
                    gpsimd.wait_ge(s_gud, 13 * ep + 3 + 2 * sp)
                gpsimd.dma_start(gu_b[g6 % 2][:],
                                 gu_w[g6 // 6, g6 % 6].rearrange("(c p) o -> p c o", p=128)).then_inc(s_w[g6 % 2], 16)
            for q in range(min(2, 4 * ne)):
                if q >= 2:
                    eq, qq = divmod(q - 2, 4)
                    gpsimd.wait_ge(s_dnd, 16 * eq + 4 * qq + 4)
                gpsimd.dma_start(dn_b[q % 2][:],
                                 dn_w[q // 4, q % 4].rearrange("(c p) d -> p c d", p=128)).then_inc(s_d[q % 2], 16)
            gpsimd.wait_ge(s_ccs, 16)
            gpsimd.collective_compute(
                "AllGather", ALU.bypass,
                ins=[cc_in[:, :]], outs=[cc_out[:, :]],
                replica_groups=[list(range(NCORES))],
            ).then_inc(s_ccs, 1)
            gpsimd.wait_ge(s_ccs, 17)
            gpsimd.dma_start(f_all[:], cc_out[:, :].rearrange("(r p) c -> p r c", p=128)).then_inc(s_fal, 16)
            gpsimd.wait_ge(s_s1, S["hc3"])
            for b, (src, c0) in enumerate([(cre_t, 0), (cre_t, 128), (cim_t, 0), (cim_t, 128)]):
                gpsimd.dma_start(ctbuf[:, b, :], src[c0:c0 + 128, :]).then_inc(s_gldg, 16)
            for e in range(ne):
                for s6 in range(6):
                    g6 = 6 * e + s6
                    if g6 < 8:
                        continue
                    ep, sp = divmod(g6 - 2, 6)
                    gpsimd.wait_ge(s_gud, 13 * ep + 3 + 2 * sp)
                    gpsimd.dma_start(gu_b[g6 % 2][:],
                                     gu_w[e, s6].rearrange("(c p) o -> p c o", p=128)).then_inc(s_w[g6 % 2], 16)
                for qi in range(4):
                    q = 4 * e + qi
                    if q < 2:
                        continue
                    eq, qq = divmod(q - 2, 4)
                    gpsimd.wait_ge(s_dnd, 16 * eq + 4 * qq + 4)
                    gpsimd.dma_start(dn_b[q % 2][:],
                                     dn_w[e, qi].rearrange("(c p) d -> p c d", p=128)).then_inc(s_d[q % 2], 16)

        @blk.tensor
        def _(tensor):
            tensor.wait_ge(s_misc, 16 * 9)
            tensor.wait_ge(s_v1, V["mr16"])
            tensor.transpose(pA[0:16, 0:128], mr16[:], idm_f32[:]).then_inc(s_t1, 1)
            tensor.wait_ge(s_mrow, 16 * 16)
            for th in range(2):
                ins = tensor.matmul(pA[:, th * 512:(th + 1) * 512], ones_f32[:],
                                    m_b[0:1, th * 512:(th + 1) * 512], start=True, stop=True)
            ins.then_inc(s_t1, 1)
            for th in range(2):
                ins = tensor.matmul(pB[:, th * 512:(th + 1) * 512], ones_f32[:],
                                    r_b[0:1, th * 512:(th + 1) * 512], start=True, stop=True)
            ins.then_inc(s_t1, 1)
            tensor.wait_ge(s_s1, S["cast15"])
            tensor.wait_ge(s_gldg, 16 * 3)
            for th in range(2):
                for k in range(DC):
                    ins = tensor.matmul(pC[0:32, th * 512:(th + 1) * 512], gw_bf[:, k, :],
                                        xn_bf[:, k, th * 512:(th + 1) * 512],
                                        start=(k == 0), stop=(k == DC - 1))
            ins.then_inc(s_t1, 1)
            for comp in range(2):
                bt = bt_re if comp == 0 else bt_im
                if comp == 1:
                    tensor.wait_ge(s_v1, V["bu_re1"])
                for ch in range(2):
                    for th in range(2):
                        for k in range(DC):
                            ins = tensor.matmul(pg[ch][:, th * 512:(th + 1) * 512],
                                                bt[:, k, ch * 128:(ch + 1) * 128],
                                                xn_bf[:, k, th * 512:(th + 1) * 512],
                                                start=(k == 0), stop=(k == DC - 1))
                    ins.then_inc(s_t1, 1)
            tensor.wait_ge(s_v1, V["lgT"])
            for tc in range(8):
                tensor.transpose(pD[0:128, tc * 128:tc * 128 + 32],
                                 lgT_sb[:, tc * 128:(tc + 1) * 128],
                                 idm_f32[0:32, 0:32]).then_inc(s_t1, 1)
            tensor.wait_ge(s_v1, V["comb"])
            for tc in range(8):
                tensor.transpose(pD[0:32, tc * 128:(tc + 1) * 128],
                                 gat2[:, tc, :], idm_f32[:]).then_inc(s_t1, 1)
            def emit_ce(e):
                W(tensor, s_ce[e % 2], 16 * (e // 2 + 1))
                if e == 0:
                    tensor.wait_ge(s_v1, V["bu_im0"])
                W(tensor, s_gue, 13 * e - 1)
                for th in range(2):
                    ins = tensor.matmul(pA[:, th * 512:(th + 1) * 512], ones_bf[:],
                                        ce_row[e % 2][0:1, th * 512:(th + 1) * 512],
                                        start=True, stop=True)
                ins.then_inc(s_gud, 1)

            def emit_gu_ocs(e, k_lo, k_hi):
                for K in range(k_lo, k_hi):
                    g6 = 6 * e + K // 2
                    if K % 2 == 0:
                        W(tensor, s_w[g6 % 2], 16 * (g6 // 2 + 1))
                    if e == 0 and K == 1:
                        tensor.wait_ge(s_v1, V["bu_im1"])
                    if e == 1 and 2 <= K <= 7:
                        tensor.wait_ge(s_sil, 5 + K)
                    else:
                        W(tensor, s_gue, (13 * e + 1) if K == 0 else (13 * e + K))
                    for th in range(2):
                        for k in range(DC):
                            ins = tensor.matmul(pg[K % 2][:, th * 512:(th + 1) * 512],
                                                gu_b[g6 % 2][:, k, (K % 2) * 128:(K % 2 + 1) * 128],
                                                xn_bf[:, k, th * 512:(th + 1) * 512],
                                                start=(k == 0), stop=(k == DC - 1))
                    ins.then_inc(s_gud, 1)

            def emit_dn(e):
                for dch in range(16):
                    q = 4 * e + dch // 4
                    if dch % 4 == 0:
                        W(tensor, s_d[q % 2], 16 * (q // 2 + 1))
                    W(tensor, s_dne, 16 * e + dch - 1)
                    for th in range(2):
                        for j in range(6):
                            if dch == 0 and th == 0:
                                W(tensor, s_gue, 13 * e + 8 + j)
                            ins = tensor.matmul(pd[dch % 2][:, th * 512:(th + 1) * 512],
                                                dn_b[q % 2][:, j, (dch % 4) * 128:(dch % 4 + 1) * 128],
                                                h_b[e % 2][:, j, th * 512:(th + 1) * 512],
                                                start=(j == 0), stop=(j == 5))
                    ins.then_inc(s_dnd, 1)
                    if dch == 3 and e >= 1 and e + 1 < ne:
                        emit_ce(e + 1)

            if ne > 0:
                emit_ce(0)
                emit_gu_ocs(0, 0, 12)
            if ne > 1:
                emit_ce(1)
                emit_gu_ocs(1, 0, 2)
            tensor.wait_ge(s_gldg, 16 * 7)
            tensor.wait_ge(s_v1, V["cneg"])
            tensor.wait_ge(s_s1, S["hc3"])
            hmap = [0, 2, 1, 3]
            for dch in range(DC):
                W(tensor, s_yev, dch - 1)
                for th in range(2):
                    for bi in range(4):
                        ins = tensor.matmul(pd[dch % 2][:, th * 512:(th + 1) * 512],
                                            ctbuf[:, bi, dch * 128:(dch + 1) * 128],
                                            H_bf[:, hmap[bi], th * 512:(th + 1) * 512],
                                            start=(bi == 0), stop=(bi == 3))
                ins.then_inc(s_t1, 1)
            if ne > 0:
                tensor.wait_ge(s_yev, 16)
                emit_dn(0)
                for e in range(1, ne):
                    emit_gu_ocs(e, 2 if e == 1 else 0, 12)
                    emit_dn(e)


        @blk.scalar
        def _(scalar):
            def ds():
                scalar.drain()
            scalar.wait_ge(s_misc, 16 * 9)
            scalar.activation(aparm[:, 0:2], nu2[:], ACT.Exp)
            ds()
            scalar.activation(aparm[:, 2:4], aparm[:, 0:2], ACT.Exp, scale=-1.0, bias=-1.0)
            scalar.activation(aparm[:, 4:6], th2[:], ACT.Exp)
            scalar.drain().then_inc(s_s1, 1)
            scalar.wait_ge(s_v1, V["stat0"])
            scalar.activation(aparm[:, 20:22], aparm[:, 18:20], ACT.Ln)
            ds()
            scalar.activation(aparm[:, 16:18], aparm[:, 20:22], ACT.Exp, scale=0.5)
            scalar.drain().then_inc(s_s1, 1)
            scalar.wait_ge(s_v1, V["var"])
            scalar.activation(mq[:, 32:40], mq[:, 32:40], ACT.Ln, bias=consts[:, 1:2])
            ds()
            scalar.activation(mq[:, 24:32], mq[:, 32:40], ACT.Exp, scale=-0.5)
            scalar.drain().then_inc(s_s1, 1)
            for k in range(16):
                scalar.wait_ge(s_v1, V[f"xn{k}"])
                scalar.activation(xn_bf[:, k, :], xn32[k % 2], ACT.Copy).then_inc(s_s1, 1)
            scalar.wait_ge(s_v1, V["ltok"])
            for tc in range(8):
                scalar.activation(gat2[:, tc, :], gat[:, tc, :], ACT.Exp,
                                  bias=gsml[:, 8 + tc:9 + tc])
            scalar.drain().then_inc(s_s1, 1)
            scalar.wait_ge(s_v1, V["ssum8"])
            scalar.activation(gsml[:, 24:32], gsml[:, 16:24], ACT.Ln)
            ds()
            scalar.activation(gsml[:, 24:32], gsml[:, 24:32], ACT.Exp, scale=-1.0)
            scalar.drain().then_inc(s_s1, 1)
            scalar.wait_ge(s_v1, V["den"])
            scalar.activation(gsml[:, 40:48], gsml[:, 40:48], ACT.Ln)
            ds()
            scalar.activation(gsml[:, 40:48], gsml[:, 40:48], ACT.Exp, scale=-1.0)
            scalar.drain().then_inc(s_s1, 1)
            scalar.activation(aparm[:, 8:10], aparm[:, 6:8], ACT.Sin)
            scalar.activation(aparm[:, 10:12], aparm[:, 22:24], ACT.Sin)
            scalar.drain().then_inc(s_s1, 1)
            for j in range(6 if ne > 0 else 0):
                scalar.wait_ge(s_gud, 2 + j)
                scalar.activation(gsil[:, j, :], pg[j % 2][:, :], ACT.Silu).then_inc(s_sil, 1)
            scalar.wait_ge(s_v1, V["hfin"])
            scalar.activation(H_bf[:, 0, :], png_re[0][:, :], ACT.Copy).then_inc(s_s1, 1)
            scalar.activation(H_bf[:, 1, :], png_im[0][:, :], ACT.Copy).then_inc(s_s1, 1)
            scalar.activation(H_bf[:, 2, :], png_re[1][:, :], ACT.Copy).then_inc(s_s1, 1)
            scalar.activation(H_bf[:, 3, :], png_im[1][:, :], ACT.Copy)
            scalar.drain().then_inc(s_s1, 1)
            for e in range(1, ne):
                for j in range(6):
                    scalar.wait_ge(s_gud, 13 * e + 2 + j)
                    scalar.activation(gsil[:, j, :], pg[j % 2][:, :], ACT.Silu).then_inc(s_sil, 1)

        @blk.vector
        def _(vector):
            def dv():
                vector.drain()
            vector.wait_ge(s_s1, S["aexp"])
            vector.tensor_scalar(aparm[:, 6:8], aparm[:, 4:6], 1.0 / TWO_PI, MAGIC, ALU.mult, ALU.add)
            dv()
            vector.tensor_scalar(aparm[:, 6:8], aparm[:, 6:8], -MAGIC, None, ALU.add)
            dv()
            vector.scalar_tensor_tensor(aparm[:, 6:8], aparm[:, 6:8], -TWO_PI, aparm[:, 4:6],
                                        ALU.mult, ALU.add)
            dv()
            vector.tensor_scalar(aparm[:, 22:24], aparm[:, 6:8], TWO_PI / 4, None, ALU.add)
            dv()
            vector.tensor_scalar(aparm[:, 20:22], aparm[:, 22:24], TWO_PI / 2, None, ALU.is_gt)
            dv()
            vector.scalar_tensor_tensor(aparm[:, 22:24], aparm[:, 20:22], -TWO_PI, aparm[:, 22:24],
                                        ALU.mult, ALU.add)
            dv()
            vector.tensor_tensor(aparm[:, 18:20], aparm[:, 2:4], aparm[:, 2:4], ALU.mult)
            dv()
            vector.tensor_scalar(aparm[:, 18:20], aparm[:, 18:20], -1.0, 1.0, ALU.mult, ALU.add)
            dv()
            for i in range(8):
                W(vector, s_tok[i % 2], 16 * (i // 2 + 1))
                vector.tensor_tensor(sq_tmp, tokstage[i % 2], tokstage[i % 2], ALU.mult)
                vector.reduce_sum(mq[:, 8 + i:9 + i], sq_tmp, AX.X)
                vector.reduce_sum(mq[:, i:i + 1], tokstage[i % 2], AX.X).then_inc(s_v1, 1)
            dv()
            vector.tensor_scalar(mq[:, 16:24], mq[:, 0:8], 1.0 / D, None, ALU.mult)
            vector.tensor_scalar(mq[:, 32:40], mq[:, 8:16], 1.0 / D, None, ALU.mult)
            dv()
            vector.tensor_tensor(mq[:, 0:8], mq[:, 16:24], mq[:, 16:24], ALU.mult)
            dv()
            vector.tensor_tensor(mq[:, 32:40], mq[:, 32:40], mq[:, 0:8], ALU.subtract)
            vector.drain().then_inc(s_v1, 1)
            vector.tensor_copy(mr16[:, 0:8], mq[:, 16:24])
            vector.wait_ge(s_s1, S["r8"])
            vector.tensor_copy(mr16[:, 8:16], mq[:, 24:32])
            vector.drain().then_inc(s_v1, 1)
            vector.wait_ge(s_t1, T1["mrT"])
            vector.tensor_copy(mr_sb[:], pA[0:16, 0:128])
            vector.drain().then_inc(s_v1, 1)
            vector.wait_ge(s_t1, T1["rbc"])
            vector.tensor_copy(m_b[:, :], pA[:])
            vector.tensor_copy(r_b[:, :], pB[:])
            vector.drain().then_inc(s_v1, 1)
            for k in range(16):
                W(vector, s_xc[k % 2], 16 * (k // 2 + 1))
                if k >= 2:
                    vector.wait_ge(s_s1, S[f"cast{k - 2}"])
                vector.tensor_tensor(xn32[k % 2], xstage[k % 2], m_b[:, :], ALU.subtract)
                vector.tensor_tensor(xn32[k % 2], xn32[k % 2], r_b[:, :], ALU.mult)
                vector.scalar_tensor_tensor(out_acc[:, k, :], xn32[k % 2], d_col[:, k:k + 1],
                                            xstage[k % 2], ALU.mult, ALU.add)
                vector.drain().then_inc(s_v1, 1)
            vector.wait_ge(s_t1, T1["lg"])
            vector.tensor_copy(lgT_sb[:], pC[0:32, :])
            vector.drain().then_inc(s_v1, 1)
            vector.wait_ge(s_t1, T1["bu_re0"])
            vector.tensor_scalar(bufA_re[:, :], pg[0][:], aparm[:, 16 + 0:17 + 0],
                                 None, ALU.mult).then_inc(s_v1, 1)
            vector.wait_ge(s_t1, T1["bu_re1"])
            vector.tensor_scalar(bufB_re[:, :], pg[1][:], aparm[:, 16 + 1:17 + 1],
                                 None, ALU.mult).then_inc(s_v1, 1)
            vector.wait_ge(s_t1, T1["gt7"])
            for tc in range(8):
                vector.tensor_copy(gat[:, tc, :], pD[0:128, tc * 128:tc * 128 + 32])
            dv()
            vector.reduce_max(gsml[:, 0:8], gat[:], AX.X)
            dv()
            vector.tensor_scalar(gsml[:, 8:16], gsml[:, 0:8], -1.0, None, ALU.mult)
            vector.drain().then_inc(s_v1, 1)
            vector.wait_ge(s_s1, S["gexp"])
            vector.reduce_sum(gsml[:, 16:24], gat2[:], AX.X)
            vector.drain().then_inc(s_v1, 1)
            vector.wait_ge(s_s1, S["inv"])
            for tc in range(8):
                vector.tensor_scalar(gat2[:, tc, :], gat2[:, tc, :], gsml[:, 24 + tc:25 + tc],
                                     None, ALU.mult)
            dv()
            for tc in range(8):
                vector.max(tv[:, tc, :], gat2[:, tc, :])
            dv()
            vector.reduce_sum(gsml[:, 16:24], tv[:], AX.X)
            vector.tensor_copy(cumb[:], tv[:])
            dv()
            for j in range(1, 8):
                vector.tensor_tensor(cumb[:, :, j:j + 1], cumb[:, :, j - 1:j],
                                     tv[:, :, j:j + 1], ALU.add)
                dv()
            vector.tensor_scalar(gsml[:, 32:40], gsml[:, 16:24], 0.8, None, ALU.mult)
            dv()
            for tc in range(8):
                vector.tensor_scalar(cumb[:, tc, :], cumb[:, tc, :], gsml[:, 32 + tc:33 + tc],
                                     None, ALU.is_lt)
            dv()
            for tc in range(8):
                vector.tensor_copy(gat3[:, tc, 0:7], cumb[:, tc, 0:7])
                vector.tensor_scalar(gat3[:, tc, 7:8], gat2[:, tc, 0:1], 0.0, None, ALU.mult)
            dv()
            for tc in range(8):
                vector.tensor_copy(cumb[:, tc, 1:8], gat3[:, tc, 0:7])
                vector.tensor_scalar(cumb[:, tc, 0:1], gat2[:, tc, 0:1], 0.0, 1.0, ALU.mult, ALU.add)
            dv()
            vector.tensor_scalar(cumb[:].rearrange("p a b -> p (a b)"),
                                 cumb[:].rearrange("p a b -> p (a b)"),
                                 -3.0e38, 3.0e38, ALU.mult, ALU.add)
            dv()
            vector.tensor_tensor(cumb[:], cumb[:], tv[:], ALU.add)
            dv()
            vector.tensor_reduce(gsml[:, 32:40], cumb[:], AX.X, ALU.min)
            dv()
            for tc in range(8):
                vector.tensor_scalar(gat3[:, tc, :], gat2[:, tc, :], gsml[:, 32 + tc:33 + tc],
                                     None, ALU.is_ge)
            dv()
            vector.tensor_tensor(gat3[:], gat2[:], gat3[:], ALU.mult)
            dv()
            vector.reduce_sum(gsml[:, 40:48], gat3[:], AX.X)
            vector.drain().then_inc(s_v1, 1)
            vector.wait_ge(s_s1, S["dinv"])
            for tc in range(8):
                vector.tensor_scalar(gat2[:, tc, :], gat3[:, tc, :],
                                     gsml[:, 40 + tc:41 + tc], None, ALU.mult)
            vector.drain().then_inc(s_v1, 1)
            vector.wait_ge(s_t1, T1["ct7"])
            for tc in range(8):
                vector.tensor_copy(combT[:, tc * 128:(tc + 1) * 128],
                                   pD[0:32, tc * 128:(tc + 1) * 128])
            vector.drain().then_inc(s_v1, 1)
            vector.wait_ge(s_t1, T1["bu_im0"])
            vector.tensor_scalar(bufA_im[:, :], pg[0][:], aparm[:, 16 + 0:17 + 0],
                                 None, ALU.mult).then_inc(s_v1, 1)
            vector.wait_ge(s_t1, T1["bu_im1"])
            vector.tensor_scalar(bufB_im[:, :], pg[1][:], aparm[:, 16 + 1:17 + 1],
                                 None, ALU.mult).then_inc(s_v1, 1)
            vector.wait_ge(s_s1, S["sins"])
            vector.tensor_tensor(aparm[:, 12:14], aparm[:, 2:4], aparm[:, 10:12], ALU.mult)
            vector.tensor_tensor(aparm[:, 14:16], aparm[:, 2:4], aparm[:, 8:10], ALU.mult)
            dv()
            vector.tensor_copy(apow_re[:, 0, :], aparm[:, 12:14])
            vector.tensor_copy(apow_im[:, 0, :], aparm[:, 14:16])
            dv()
            for k in range(10):
                re_k = apow_re[:, k, :]; im_k = apow_im[:, k, :]
                vector.tensor_tensor(aparm[:, 20:22], re_k, re_k, ALU.mult)
                vector.tensor_tensor(aparm[:, 22:24], im_k, im_k, ALU.mult)
                dv()
                vector.tensor_tensor(apow_re[:, k + 1, :], aparm[:, 20:22], aparm[:, 22:24],
                                     ALU.subtract)
                vector.tensor_tensor(aparm[:, 20:22], re_k, im_k, ALU.mult)
                dv()
                vector.tensor_scalar(apow_im[:, k + 1, :], aparm[:, 20:22], 2.0, None, ALU.mult)
                dv()
            vector.tensor_scalar(napow_im[:].rearrange("p a b -> p (a b)"),
                                 apow_im[:].rearrange("p a b -> p (a b)"), -1.0, None, ALU.mult)
            vector.tensor_scalar(qp_re[:, 0, :], apow_re[:, 0, :], 0.0, 1.0, ALU.mult, ALU.add)
            vector.tensor_scalar(qp_im[:, 0, :], apow_re[:, 0, :], 0.0, None, ALU.mult)
            dv()
            for p in range(7):
                a10r = apow_re[:, 10, :]; a10i = apow_im[:, 10, :]
                vector.tensor_tensor(aparm[:, 20:22], qp_re[:, p, :], a10r, ALU.mult)
                vector.tensor_tensor(aparm[:, 22:24], qp_im[:, p, :], a10i, ALU.mult)
                dv()
                vector.tensor_tensor(qp_re[:, p + 1, :], aparm[:, 20:22], aparm[:, 22:24],
                                     ALU.subtract)
                vector.tensor_tensor(aparm[:, 20:22], qp_re[:, p, :], a10i, ALU.mult)
                dv()
                vector.tensor_tensor(aparm[:, 22:24], qp_im[:, p, :], a10r, ALU.mult)
                dv()
                vector.tensor_tensor(qp_im[:, p + 1, :], aparm[:, 20:22], aparm[:, 22:24],
                                     ALU.add)
                dv()
            vector.tensor_copy(qp_re[0:1, 0, 0:1], qp_re[0:1, 0, 0:1])
            vector.drain().then_inc(s_v1, 1)
            def e0_evacs(k):
                if ne == 0:
                    return
                sched = {0: [("ce", 0)], 1: [("g", 0)], 2: [("g", 1), ("g", 2)],
                         3: [("g", 3), ("g", 4)], 4: [("g", 5), ("h", 0)],
                         5: [("h", 1), ("h", 2), ("h", 3)],
                         6: [("h", 4), ("h", 5)]}
                for kind, j in sched.get(k, []):
                    if kind == "ce":
                        vector.wait_ge(s_gud, 1)
                        vector.tensor_copy(ce_b[0][:, :], pA[:]).then_inc(s_gue, 1)
                    elif kind == "g":
                        vector.wait_ge(s_sil, j + 1)
                        vector.tensor_tensor(gsil[:, j, :], gsil[:, j, :], ce_b[0][:, :],
                                             ALU.mult).then_inc(s_gue, 1)
                    else:
                        vector.wait_ge(s_gud, 8 + j)
                        vector.tensor_tensor(h_b[0][:, j, :], gsil[:, j, :], pg[j % 2][:],
                                             ALU.mult).then_inc(s_gue, 1)

            vector.wait_ge(s_t1, T1["bu_im1"])
            cur_re = [bufA_re, bufB_re]; cur_im = [bufA_im, bufB_im]
            nxt_re = [png_re[0], png_re[1]]; nxt_im = [png_im[0], png_im[1]]
            for k in range(7):
                e0_evacs(k)
                dstep = 1 << k
                n = T - dstep
                for ch in range(2):
                    sre = apow_re[:, k, ch:ch + 1]
                    sim_ = apow_im[:, k, ch:ch + 1]
                    nsim = napow_im[:, k, ch:ch + 1]
                    a_re, a_im = cur_re[ch], cur_im[ch]
                    b_re, b_im = nxt_re[ch], nxt_im[ch]
                    vector.scalar_tensor_tensor(b_re[:, dstep:T], a_re[:, 0:n], sre,
                                                a_re[:, dstep:T], ALU.mult, ALU.add)
                    vector.scalar_tensor_tensor(b_re[:, dstep:T], a_im[:, 0:n], nsim,
                                                b_re[:, dstep:T], ALU.mult, ALU.add)
                    vector.scalar_tensor_tensor(b_im[:, dstep:T], a_im[:, 0:n], sre,
                                                a_im[:, dstep:T], ALU.mult, ALU.add)
                    vector.scalar_tensor_tensor(b_im[:, dstep:T], a_re[:, 0:n], sim_,
                                                b_im[:, dstep:T], ALU.mult, ALU.add)
                    vector.tensor_copy(b_re[:, 0:dstep], a_re[:, 0:dstep])
                    vector.tensor_copy(b_im[:, 0:dstep], a_im[:, 0:dstep])
                    dv()
                cur_re, nxt_re = nxt_re, cur_re
                cur_im, nxt_im = nxt_im, cur_im
            for ch in range(2):
                vector.tensor_copy(f_sb[:, 2 * ch:2 * ch + 1], cur_re[ch][:, T - 1:T])
                vector.tensor_copy(f_sb[:, 2 * ch + 1:2 * ch + 2], cur_im[ch][:, T - 1:T])
            vector.drain().then_inc(s_v1, 1)
            vector.wait_ge(s_fal, 16)
            for p in range(7):
                for j in range(7):
                    vector.tensor_scalar(gtmp[:, p, :, j], f_all[:, j, :],
                                         selc[:, p * 8 + j:p * 8 + j + 1], None, ALU.mult)
            dv()
            vector.tensor_reduce(gacc[:, 0:7, :], gtmp[:], AX.X, ALU.add)
            vector.tensor_scalar(gacc[:, 7, :], f_all[:, 7, :], 0.0, None, ALU.mult)
            dv()
            for p in range(8):
                for ch in range(2):
                    gre = gacc[:, p, 2 * ch:2 * ch + 1]
                    gim = gacc[:, p, 2 * ch + 1:2 * ch + 2]
                    qre = qp_re[:, p, ch:ch + 1]
                    qim = qp_im[:, p, ch:ch + 1]
                    vector.tensor_scalar(qg[:, p, ch, 0:1], gre, qre, None, ALU.mult)
                    vector.tensor_scalar(qg[:, p, ch, 1:2], gim, qim, None, ALU.mult)
                    vector.tensor_scalar(qg[:, p, ch, 2:3], gim, qre, None, ALU.mult)
                    vector.tensor_scalar(qg[:, p, ch, 3:4], gre, qim, None, ALU.mult)
            dv()
            for ch in range(2):
                for t4 in range(4):
                    vector.tensor_reduce(hsum[:, ch, t4:t4 + 1],
                                         qg[:, :, ch, t4:t4 + 1].rearrange("p a b -> p (a b)"),
                                         AX.X, ALU.add)
            dv()
            for ch in range(2):
                vector.tensor_tensor(hin[:, 2 * ch:2 * ch + 1], hsum[:, ch, 0:1],
                                     hsum[:, ch, 1:2], ALU.subtract)
                vector.tensor_tensor(hin[:, 2 * ch + 1:2 * ch + 2], hsum[:, ch, 2:3],
                                     hsum[:, ch, 3:4], ALU.add)
            dv()
            if ne > 1:
                vector.wait_ge(s_gud, 14)
                vector.tensor_copy(ce_b[1][:, :], pA[:]).then_inc(s_gue, 1)
            corr_re = [bufA_re, bufB_re]; corr_im = [bufA_im, bufB_im]
            for ch in range(2):
                c_re, c_im = corr_re[ch], corr_im[ch]
                hre = hin[:, 2 * ch:2 * ch + 1]; him = hin[:, 2 * ch + 1:2 * ch + 2]
                are = apow_re[:, 0, ch:ch + 1]; aim = apow_im[:, 0, ch:ch + 1]
                vector.tensor_scalar(c_re[:, 0:1], hre, are, None, ALU.mult)
                vector.tensor_scalar(aparm[:, 20:21], him, aim, None, ALU.mult)
                dv()
                vector.tensor_tensor(c_re[:, 0:1], c_re[:, 0:1], aparm[:, 20:21], ALU.subtract)
                vector.tensor_scalar(c_im[:, 0:1], him, are, None, ALU.mult)
                vector.tensor_scalar(aparm[:, 20:21], hre, aim, None, ALU.mult)
                dv()
                vector.tensor_tensor(c_im[:, 0:1], c_im[:, 0:1], aparm[:, 20:21], ALU.add)
                dv()
                for k in range(7):
                    dstep = 1 << k
                    hi = min(2 * dstep, T)
                    n = hi - dstep
                    sre = apow_re[:, k, ch:ch + 1]
                    sim_ = apow_im[:, k, ch:ch + 1]
                    nsim = napow_im[:, k, ch:ch + 1]
                    vector.tensor_scalar(c_re[:, dstep:hi], c_re[:, 0:n], sre, None, ALU.mult)
                    dv()
                    vector.scalar_tensor_tensor(c_re[:, dstep:hi], c_im[:, 0:n], nsim,
                                                c_re[:, dstep:hi], ALU.mult, ALU.add)
                    vector.tensor_scalar(c_im[:, dstep:hi], c_im[:, 0:n], sre, None, ALU.mult)
                    dv()
                    vector.scalar_tensor_tensor(c_im[:, dstep:hi], c_re[:, 0:n], sim_,
                                                c_im[:, dstep:hi], ALU.mult, ALU.add)
                    dv()
                vector.tensor_tensor(cur_re[ch][:, 0:128], cur_re[ch][:, 0:128],
                                     c_re[:, 0:128], ALU.add)
                vector.tensor_tensor(cur_im[ch][:, 0:128], cur_im[ch][:, 0:128],
                                     c_im[:, 0:128], ALU.add)
            vector.drain().then_inc(s_v1, 1)
            vector.wait_ge(s_gldg, 16 * 7)
            vector.tensor_scalar(ctbuf[:, 2, :], ctbuf[:, 2, :], -1.0, None, ALU.mult)
            vector.tensor_scalar(ctbuf[:, 3, :], ctbuf[:, 3, :], -1.0, None, ALU.mult)
            vector.drain().then_inc(s_v1, 1)
            for dch in range(DC):
                vector.wait_ge(s_t1, T1[f"y{dch}"])
                vector.tensor_tensor(out_acc[:, dch, :], out_acc[:, dch, :],
                                     pd[dch % 2][:], ALU.add).then_inc(s_yev, 1)
            def emit_ce_evac(e):
                vector.wait_ge(s_gud, 13 * e + 1)
                vector.tensor_copy(ce_b[e % 2][:, :], pA[:]).then_inc(s_gue, 1)

            for e in range(ne):
                if e >= 1:
                    for j in range(6):
                        vector.wait_ge(s_sil, 6 * e + j + 1)
                        vector.tensor_tensor(gsil[:, j, :], gsil[:, j, :], ce_b[e % 2][:, :],
                                             ALU.mult).then_inc(s_gue, 1)
                    for j in range(6):
                        if j == 0:
                            W(vector, s_dnd, 16 * (e - 1))
                        vector.wait_ge(s_gud, 13 * e + 8 + j)
                        vector.tensor_tensor(h_b[e % 2][:, j, :], gsil[:, j, :], pg[j % 2][:],
                                             ALU.mult).then_inc(s_gue, 1)
                for dch in range(16):
                    vector.wait_ge(s_dnd, 16 * e + dch + 1)
                    vector.tensor_tensor(out_acc[:, dch, :], out_acc[:, dch, :],
                                         pd[dch % 2][:], ALU.add).then_inc(s_dne, 1)
                    if dch == 3 and e >= 1 and e + 1 < ne:
                        emit_ce_evac(e + 1)

    return nc


_NC_CACHE = None


def _get_nc():
    global _NC_CACHE
    if _NC_CACHE is None:
        _NC_CACHE = build_nc()
    return _NC_CACHE


def kernel(L0_attn_x, L0_attn_nu_log, L0_attn_theta_log, L0_attn_B_re, L0_attn_B_im,
           L0_attn_C_re, L0_attn_C_im, L0_attn_D, L0_ffn_gate_weight,
           L0_ffn_gate_up_proj, L0_ffn_down_proj):
    C = np.ascontiguousarray
    x = np.asarray(L0_attn_x, np.float32).reshape(8192, 2048)
    xT_full = C(x.T)
    nu_t = C(np.asarray(L0_attn_nu_log, np.float32).reshape(2, 128).T)
    theta_t = C(np.asarray(L0_attn_theta_log, np.float32).reshape(2, 128).T)
    d_t = C(np.asarray(L0_attn_D, np.float32).reshape(16, 128).T)
    bre_t = C(np.asarray(L0_attn_B_re, np.float32).T)
    bim_t = C(np.asarray(L0_attn_B_im, np.float32).T)
    cre_t = C(np.asarray(L0_attn_C_re, np.float32).T)
    cim_t = C(np.asarray(L0_attn_C_im, np.float32).T)
    gw_t = C(np.asarray(L0_ffn_gate_weight, np.float32).T)
    gup = np.asarray(L0_ffn_gate_up_proj, np.float32)
    gu_w = C(gup.transpose(0, 2, 1).reshape(NE, D, 6, 256).transpose(0, 2, 1, 3))
    dnp = np.asarray(L0_ffn_down_proj, np.float32)
    dn_w = C(dnp.transpose(0, 2, 1).reshape(NE, 768, 4, 512).transpose(0, 2, 1, 3))
    idm_bf = np.eye(128, dtype=ml_dtypes.bfloat16)
    idm_f32 = np.eye(128, dtype=np.float32)
    ones_bf = np.ones((1, 128), dtype=ml_dtypes.bfloat16)
    ones_f32_np = np.ones((1, 128), dtype=np.float32)
    consts_np = np.tile(np.array([[-1.0, 1e-5]], np.float32), (128, 1))

    in_maps = []
    for c in range(NCORES):
        selm = np.zeros((8, 8), np.float32)
        for j in range(c):
            selm[c - 1 - j, j] = 1.0
        selc_np = np.broadcast_to(selm.reshape(1, 64), (128, 64)).copy()
        in_maps.append(dict(
            xT=C(xT_full[:, c * T:(c + 1) * T]),
            xtok=C(x[c * T:(c + 1) * T, :]),
            nu_t=nu_t, theta_t=theta_t, d_t=d_t, selc_t=selc_np,
            bre_t=bre_t, bim_t=bim_t, cre_t=cre_t, cim_t=cim_t,
            gw_t=gw_t, gu_w=gu_w, dn_w=dn_w,
            idm_bf=idm_bf, idm_f32=idm_f32, ones_bf=ones_bf,
            ones_f32=ones_f32_np, consts_d=consts_np,
        ))
    nc = _get_nc()
    res = run_bass_kernel_spmd(nc, in_maps, core_ids=list(range(NCORES)))
    outT = np.concatenate([np.asarray(r["oT"]) for r in res.results], axis=1)
    return C(outT.T).reshape(1, 8192, 2048)


# version 08
# speedup vs baseline: 1.0000x; 1.0000x over previous
import numpy as np
import ml_dtypes
import concourse.bass as bass
import concourse.mybir as mybir
from concourse.bass_utils import run_bass_kernel_spmd

F32 = mybir.dt.float32
BF16 = mybir.dt.bfloat16
U8 = mybir.dt.uint8
ACT = mybir.ActivationFunctionType
ALU = mybir.AluOpType
AX = mybir.AxisListType

NCORES = 8
T = 1024
D = 2048
DC = 16
NE = 32
TWO_PI = 6.283185307179586
MAGIC = 8388608.0

def _marks(names):
    return {n: i + 1 for i, n in enumerate(names)}

V = _marks(
    [f"stat{i}" for i in range(8)] + ["var", "mr16", "mrsb", "mbrb"]
    + [f"xn{k}" for k in range(16)]
    + ["lgT", "bu_re0", "bu_re1", "ltok", "ssum8", "den", "comb", "combT",
       "bu_im0", "bu_im1", "tables", "f", "hfin", "cneg"]
)
S = _marks(
    ["aexp"] + [f"sq{i}" for i in range(8)] + ["gamma", "r8"]
    + [f"cast{k}" for k in range(16)]
    + ["gexp", "inv", "dinv", "sins", "hc0", "hc1", "hc2", "hc3"]
)
T1 = _marks(
    ["mrT", "mbc", "rbc", "lg"]
    + ["bu_re0", "bu_re1", "bu_im0", "bu_im1"]
    + [f"gt{t}" for t in range(8)]
    + [f"ct{t}" for t in range(8)]
    + [f"y{d}" for d in range(16)]
)


def W(eng, sem, n):
    if n > 0:
        eng.wait_ge(sem, n)


def build_nc(ne=NE, stage=None):
    nc = bass.Bass("TRN2", target_bir_lowering=False, detect_race_conditions=False)

    xT = nc.dram_tensor("xT", [D, T], F32, kind="ExternalInput")
    xtok = nc.dram_tensor("xtok", [T, D], F32, kind="ExternalInput")
    nu_t = nc.dram_tensor("nu_t", [128, 2], F32, kind="ExternalInput")
    theta_t = nc.dram_tensor("theta_t", [128, 2], F32, kind="ExternalInput")
    d_t = nc.dram_tensor("d_t", [128, 16], F32, kind="ExternalInput")
    selc_t = nc.dram_tensor("selc_t", [128, 64], F32, kind="ExternalInput")
    bre_t = nc.dram_tensor("bre_t", [D, 256], F32, kind="ExternalInput")
    bim_t = nc.dram_tensor("bim_t", [D, 256], F32, kind="ExternalInput")
    cre_t = nc.dram_tensor("cre_t", [256, D], F32, kind="ExternalInput")
    cim_t = nc.dram_tensor("cim_t", [256, D], F32, kind="ExternalInput")
    gw_t = nc.dram_tensor("gw_t", [D, NE], F32, kind="ExternalInput")
    gu_w = nc.dram_tensor("gu_w", [NE, 6, D, 256], F32, kind="ExternalInput")
    dn_w = nc.dram_tensor("dn_w", [NE, 4, 768, 512], F32, kind="ExternalInput")
    idm_bf_d = nc.dram_tensor("idm_bf", [128, 128], BF16, kind="ExternalInput")
    idm_f32_d = nc.dram_tensor("idm_f32", [128, 128], F32, kind="ExternalInput")
    ones_bf_d = nc.dram_tensor("ones_bf", [1, 128], BF16, kind="ExternalInput")
    ones_f32_d = nc.dram_tensor("ones_f32", [1, 128], F32, kind="ExternalInput")
    consts_d = nc.dram_tensor("consts_d", [128, 2], F32, kind="ExternalInput")
    oT = nc.dram_tensor("oT", [D, T], F32, kind="ExternalOutput")
    cc_in = nc.dram_tensor("cc_in", [128, 4], F32, kind="Internal")
    cc_out = nc.dram_tensor("cc_out", [128 * NCORES, 4], F32, kind="Internal")

    from contextlib import ExitStack
    ctx = ExitStack()
    with ctx:
        E = ctx.enter_context
        out_acc = E(nc.sbuf_tensor("out_acc", [128, DC, T], F32))
        xn_bf = E(nc.sbuf_tensor("xn_bf", [128, DC, T], BF16))
        gu_b = [E(nc.sbuf_tensor(f"gu_b{i}", [128, 16, 256], BF16)) for i in range(2)]
        dn_b = [E(nc.sbuf_tensor(f"dn_b{i}", [128, 6, 512], BF16)) for i in range(2)]
        gw_bf = E(nc.sbuf_tensor("gw_bf", [128, 16, NE], BF16))
        idm_bf = E(nc.sbuf_tensor("idm_bf_s", [128, 128], BF16))
        idm_f32 = E(nc.sbuf_tensor("idm_f32_s", [128, 128], F32))
        ones_bf = E(nc.sbuf_tensor("ones_bf_s", [1, 128], BF16))
        ones_f32 = E(nc.sbuf_tensor("ones_f32_s", [1, 128], F32))
        nu2 = E(nc.sbuf_tensor("nu2", [128, 2], F32))
        th2 = E(nc.sbuf_tensor("th2", [128, 2], F32))
        d_col = E(nc.sbuf_tensor("d_col", [128, 16], F32))
        selc = E(nc.sbuf_tensor("selc", [128, 64], F32))
        consts = E(nc.sbuf_tensor("consts", [128, 2], F32))
        aparm = E(nc.sbuf_tensor("aparm", [128, 24], F32))
        apow_re = E(nc.sbuf_tensor("apow_re", [128, 11, 2], F32))
        apow_im = E(nc.sbuf_tensor("apow_im", [128, 11, 2], F32))
        napow_im = E(nc.sbuf_tensor("napow_im", [128, 11, 2], F32))
        qp_re = E(nc.sbuf_tensor("qp_re", [128, 8, 2], F32))
        qp_im = E(nc.sbuf_tensor("qp_im", [128, 8, 2], F32))
        f_sb = E(nc.sbuf_tensor("f_sb", [128, 4], F32))
        f_all = E(nc.sbuf_tensor("f_all", [128, 8, 4], F32))
        gacc = E(nc.sbuf_tensor("gacc", [128, 8, 4], F32))
        gtmp = E(nc.sbuf_tensor("gtmp", [128, 7, 4, 7], F32))
        qg = E(nc.sbuf_tensor("qg", [128, 8, 2, 4], F32))
        hsum = E(nc.sbuf_tensor("hsum", [128, 2, 4], F32))
        hin = E(nc.sbuf_tensor("hin", [128, 4], F32))
        ce_row = [E(nc.sbuf_tensor(f"ce_row{i}", [1, T], BF16)) for i in range(2)]
        mq = E(nc.sbuf_tensor("mq", [128, 40], F32))
        mr16 = E(nc.sbuf_tensor("mr16", [128, 16], F32))
        mr_sb = E(nc.sbuf_tensor("mr_sb", [16, 128], F32))
        gat = E(nc.sbuf_tensor("gat", [128, 8, 32], F32))
        gat2 = E(nc.sbuf_tensor("gat2", [128, 8, 32], F32))
        gat3 = E(nc.sbuf_tensor("gat3", [128, 8, 32], F32))
        tv = E(nc.sbuf_tensor("tv", [128, 8, 8], F32))
        cumb = E(nc.sbuf_tensor("cumb", [128, 8, 8], F32))
        gsml = E(nc.sbuf_tensor("gsml", [128, 48], F32))
        combT = E(nc.sbuf_tensor("combT", [32, T], BF16))
        h0_p = E(nc.sbuf_tensor("h0_p", [128, 6, T], BF16))
        gsil_p = E(nc.sbuf_tensor("gsil_p", [128, 6, T], BF16))
        ceb0_p = E(nc.sbuf_tensor("ceb0_p", [128, T], BF16))

        arena = E(nc.sbuf_tensor("arena", [128, 40960], U8))

        def afr(byte0, nbytes, dt=F32):
            return arena[:, byte0:byte0 + nbytes].bitcast(dt)

        tokstage = [afr(0, 8192), afr(8192, 8192)]
        sq2 = [afr(16384, 8192), afr(24576, 8192)]
        xstage = [afr(0, 4096), afr(4096, 4096)]
        xn32 = [afr(8192, 4096), afr(12288, 4096)]
        m_b = afr(16384, 4096)
        r_b = afr(20480, 4096)
        bt_re = afr(0, 8192, BF16).rearrange("p (c n) -> p c n", c=16)
        bt_im = afr(8192, 8192, BF16).rearrange("p (c n) -> p c n", c=16)
        bufA_re = afr(16384, 4096)
        bufA_im = afr(20480, 4096)
        bufB_re = afr(24576, 4096)
        bufB_im = afr(28672, 4096)
        lgT_sb = afr(32768, 4096)[0:32, :]
        png_re = [afr(0, 4096), afr(8192, 4096)]
        png_im = [afr(4096, 4096), afr(12288, 4096)]
        H_bf = afr(32768, 8192, BF16).rearrange("p (b t) -> p b t", b=4)
        ctbuf = afr(0, 16384, BF16).rearrange("p (b t) -> p b t", b=4)
        h_b = [h0_p[:], afr(12288, 12288, BF16).rearrange("p (j t) -> p j t", j=6)]
        gsil = gsil_p[:]
        ce_b = [ceb0_p[:], afr(18432, 2048, BF16)]

        pA = E(nc.psum_tensor("pA", [128, 1024], F32))
        pB = E(nc.psum_tensor("pB", [128, 1024], F32))
        pC = E(nc.psum_tensor("pC", [128, 1024], F32))
        pD = E(nc.psum_tensor("pD", [128, 1024], F32))
        pg = [pA, pB]
        pd = [pC, pD]

        s_misc = E(nc.semaphore("s_misc"))
        s_gldg = E(nc.semaphore("s_gldg"))
        s_tok = [E(nc.semaphore(f"s_tok{i}")) for i in range(2)]
        s_xc = [E(nc.semaphore(f"s_xc{i}")) for i in range(2)]
        s_w = [E(nc.semaphore(f"s_w{i}")) for i in range(2)]
        s_d = [E(nc.semaphore(f"s_d{i}")) for i in range(2)]
        s_ce = [E(nc.semaphore(f"s_ce{i}")) for i in range(2)]
        s_mrow = E(nc.semaphore("s_mrow"))
        s_ccs = E(nc.semaphore("s_ccs"))
        s_fal = E(nc.semaphore("s_fal"))
        s_v1 = E(nc.semaphore("s_v1"))
        s_s1 = E(nc.semaphore("s_s1"))
        s_t1 = E(nc.semaphore("s_t1"))
        s_gud = E(nc.semaphore("s_gud"))
        s_gue = E(nc.semaphore("s_gue"))
        s_sil = E(nc.semaphore("s_sil"))
        s_dnd = E(nc.semaphore("s_dnd"))
        s_dne = E(nc.semaphore("s_dne"))
        s_yev = E(nc.semaphore("s_yev"))
        s_st = E(nc.semaphore("s_st"))

        nc.const_aps.register(F32, -1.0, consts[:, 0:1])

        blk = E(nc.Block())

        @blk.sync
        def _(sync):
            sync.dma_start(nu2[:], nu_t[:, :]).then_inc(s_misc, 16)
            sync.dma_start(th2[:], theta_t[:, :]).then_inc(s_misc, 16)
            sync.dma_start(d_col[:], d_t[:, :]).then_inc(s_misc, 16)
            sync.dma_start(selc[:], selc_t[:, :]).then_inc(s_misc, 16)
            sync.dma_start(idm_bf[:], idm_bf_d[:, :]).then_inc(s_misc, 16)
            sync.dma_start(idm_f32[:], idm_f32_d[:, :]).then_inc(s_misc, 16)
            sync.dma_start(ones_bf[:], ones_bf_d[:, :]).then_inc(s_misc, 16)
            sync.dma_start(ones_f32[:], ones_f32_d[:, :]).then_inc(s_misc, 16)
            sync.dma_start(consts[:], consts_d[:, :]).then_inc(s_misc, 16)
            for i in range(8):
                if i >= 2:
                    sync.wait_ge(s_v1, V[f"stat{i - 2}"])
                sync.dma_start(tokstage[i % 2], xtok[i * 128:(i + 1) * 128, :]).then_inc(s_tok[i % 2], 16)
            sync.wait_ge(s_v1, V["mrsb"])
            sync.dma_start(m_b[0:1, 0:1024], mr_sb[0:8, 0:128]).then_inc(s_mrow, 16)
            sync.dma_start(r_b[0:1, 0:1024], mr_sb[8:16, 0:128]).then_inc(s_mrow, 16)
            for k in range(16):
                if k < 2:
                    sync.wait_ge(s_v1, V["mbrb"])
                else:
                    sync.wait_ge(s_v1, V[f"xn{k - 2}"])
                sync.dma_start(xstage[k % 2], xT[k * 128:(k + 1) * 128, :]).then_inc(s_xc[k % 2], 16)
            for e in range(min(2, ne)):
                sync.wait_ge(s_v1, V["combT"])
                sync.dma_start(ce_row[e % 2][:], combT[e:e + 1, :]).then_inc(s_ce[e % 2], 16)
            sync.wait_ge(s_v1, V["f"])
            sync.dma_start(cc_in[:, :], f_sb[:]).then_inc(s_ccs, 16)
            for e in range(2, ne):
                sync.wait_ge(s_gud, 13 * (e - 2) + 1)
                sync.dma_start(ce_row[e % 2][:], combT[e:e + 1, :]).then_inc(s_ce[e % 2], 16)
            if stage == "xn":
                sync.wait_ge(s_v1, V["xn15"])
            elif stage == "hfin":
                sync.wait_ge(s_v1, V["hfin"])
            elif ne > 0:
                sync.wait_ge(s_dne, 16 * ne)
            else:
                sync.wait_ge(s_yev, 16)
            sync.dma_start(oT[:, :].rearrange("(c p) t -> p c t", p=128), out_acc[:]).then_inc(s_st, 16)
            sync.wait_ge(s_st, 16)

        @blk.gpsimd
        def _(gpsimd):
            gpsimd.dma_start(gw_bf[:], gw_t[:, :].rearrange("(c p) e -> p c e", p=128)).then_inc(s_gldg, 16)
            gpsimd.wait_ge(s_v1, V["xn15"])
            gpsimd.wait_ge(s_s1, S["cast15"])
            gpsimd.dma_start(bt_re, bre_t[:, :].rearrange("(c p) n -> p c n", p=128)).then_inc(s_gldg, 16)
            gpsimd.dma_start(bt_im, bim_t[:, :].rearrange("(c p) n -> p c n", p=128)).then_inc(s_gldg, 16)
            for g6 in range(min(8, 6 * ne)):
                if g6 >= 2:
                    ep, sp = divmod(g6 - 2, 6)
                    gpsimd.wait_ge(s_gud, 13 * ep + 3 + 2 * sp)
                gpsimd.dma_start(gu_b[g6 % 2][:],
                                 gu_w[g6 // 6, g6 % 6].rearrange("(c p) o -> p c o", p=128)).then_inc(s_w[g6 % 2], 16)
            for q in range(min(2, 4 * ne)):
                if q >= 2:
                    eq, qq = divmod(q - 2, 4)
                    gpsimd.wait_ge(s_dnd, 16 * eq + 4 * qq + 4)
                gpsimd.dma_start(dn_b[q % 2][:],
                                 dn_w[q // 4, q % 4].rearrange("(c p) d -> p c d", p=128)).then_inc(s_d[q % 2], 16)
            gpsimd.wait_ge(s_ccs, 16)
            gpsimd.collective_compute(
                "AllGather", ALU.bypass,
                ins=[cc_in[:, :]], outs=[cc_out[:, :]],
                replica_groups=[list(range(NCORES))],
            ).then_inc(s_ccs, 1)
            gpsimd.wait_ge(s_ccs, 17)
            gpsimd.dma_start(f_all[:], cc_out[:, :].rearrange("(r p) c -> p r c", p=128)).then_inc(s_fal, 16)
            gpsimd.wait_ge(s_s1, S["hc3"])
            for b, (src, c0) in enumerate([(cre_t, 0), (cre_t, 128), (cim_t, 0), (cim_t, 128)]):
                gpsimd.dma_start(ctbuf[:, b, :], src[c0:c0 + 128, :]).then_inc(s_gldg, 16)
            for e in range(ne):
                for s6 in range(6):
                    g6 = 6 * e + s6
                    if g6 < 8:
                        continue
                    ep, sp = divmod(g6 - 2, 6)
                    gpsimd.wait_ge(s_gud, 13 * ep + 3 + 2 * sp)
                    gpsimd.dma_start(gu_b[g6 % 2][:],
                                     gu_w[e, s6].rearrange("(c p) o -> p c o", p=128)).then_inc(s_w[g6 % 2], 16)
                for qi in range(4):
                    q = 4 * e + qi
                    if q < 2:
                        continue
                    eq, qq = divmod(q - 2, 4)
                    gpsimd.wait_ge(s_dnd, 16 * eq + 4 * qq + 4)
                    gpsimd.dma_start(dn_b[q % 2][:],
                                     dn_w[e, qi].rearrange("(c p) d -> p c d", p=128)).then_inc(s_d[q % 2], 16)

        @blk.tensor
        def _(tensor):
            tensor.wait_ge(s_misc, 16 * 9)
            tensor.wait_ge(s_v1, V["mr16"])
            tensor.transpose(pA[0:16, 0:128], mr16[:], idm_f32[:]).then_inc(s_t1, 1)
            tensor.wait_ge(s_mrow, 16 * 2)
            for th in range(2):
                ins = tensor.matmul(pA[:, th * 512:(th + 1) * 512], ones_f32[:],
                                    m_b[0:1, th * 512:(th + 1) * 512], start=True, stop=True)
            ins.then_inc(s_t1, 1)
            for th in range(2):
                ins = tensor.matmul(pB[:, th * 512:(th + 1) * 512], ones_f32[:],
                                    r_b[0:1, th * 512:(th + 1) * 512], start=True, stop=True)
            ins.then_inc(s_t1, 1)
            tensor.wait_ge(s_s1, S["cast15"])
            tensor.wait_ge(s_gldg, 16 * 1)
            for th in range(2):
                for k in range(DC):
                    ins = tensor.matmul(pC[0:32, th * 512:(th + 1) * 512], gw_bf[:, k, :],
                                        xn_bf[:, k, th * 512:(th + 1) * 512],
                                        start=(k == 0), stop=(k == DC - 1))
            ins.then_inc(s_t1, 1)
            tensor.wait_ge(s_gldg, 16 * 3)
            for comp in range(2):
                bt = bt_re if comp == 0 else bt_im
                if comp == 1:
                    tensor.wait_ge(s_v1, V["bu_re1"])
                for ch in range(2):
                    for th in range(2):
                        for k in range(DC):
                            ins = tensor.matmul(pg[ch][:, th * 512:(th + 1) * 512],
                                                bt[:, k, ch * 128:(ch + 1) * 128],
                                                xn_bf[:, k, th * 512:(th + 1) * 512],
                                                start=(k == 0), stop=(k == DC - 1))
                    ins.then_inc(s_t1, 1)
            tensor.wait_ge(s_v1, V["lgT"])
            for tc in range(8):
                tensor.transpose(pD[0:128, tc * 128:tc * 128 + 32],
                                 lgT_sb[:, tc * 128:(tc + 1) * 128],
                                 idm_f32[0:32, 0:32]).then_inc(s_t1, 1)
            tensor.wait_ge(s_v1, V["comb"])
            for tc in range(8):
                tensor.transpose(pD[0:32, tc * 128:(tc + 1) * 128],
                                 gat2[:, tc, :], idm_f32[:]).then_inc(s_t1, 1)
            def emit_ce(e):
                W(tensor, s_ce[e % 2], 16 * (e // 2 + 1))
                if e == 0:
                    tensor.wait_ge(s_v1, V["bu_im0"])
                W(tensor, s_gue, 13 * e - 1)
                for th in range(2):
                    ins = tensor.matmul(pA[:, th * 512:(th + 1) * 512], ones_bf[:],
                                        ce_row[e % 2][0:1, th * 512:(th + 1) * 512],
                                        start=True, stop=True)
                ins.then_inc(s_gud, 1)

            def emit_gu_ocs(e, k_lo, k_hi):
                for K in range(k_lo, k_hi):
                    g6 = 6 * e + K // 2
                    if K % 2 == 0:
                        W(tensor, s_w[g6 % 2], 16 * (g6 // 2 + 1))
                    if e == 0 and K == 1:
                        tensor.wait_ge(s_v1, V["bu_im1"])
                    if e == 1 and 2 <= K <= 7:
                        tensor.wait_ge(s_sil, 5 + K)
                    else:
                        W(tensor, s_gue, (13 * e + 1) if K == 0 else (13 * e + K))
                    for th in range(2):
                        for k in range(DC):
                            ins = tensor.matmul(pg[K % 2][:, th * 512:(th + 1) * 512],
                                                gu_b[g6 % 2][:, k, (K % 2) * 128:(K % 2 + 1) * 128],
                                                xn_bf[:, k, th * 512:(th + 1) * 512],
                                                start=(k == 0), stop=(k == DC - 1))
                    ins.then_inc(s_gud, 1)

            def emit_dn(e):
                for dch in range(16):
                    q = 4 * e + dch // 4
                    if dch % 4 == 0:
                        W(tensor, s_d[q % 2], 16 * (q // 2 + 1))
                    W(tensor, s_dne, 16 * e + dch - 1)
                    for th in range(2):
                        for j in range(6):
                            if dch == 0 and th == 0:
                                W(tensor, s_gue, 13 * e + 8 + j)
                            ins = tensor.matmul(pd[dch % 2][:, th * 512:(th + 1) * 512],
                                                dn_b[q % 2][:, j, (dch % 4) * 128:(dch % 4 + 1) * 128],
                                                h_b[e % 2][:, j, th * 512:(th + 1) * 512],
                                                start=(j == 0), stop=(j == 5))
                    ins.then_inc(s_dnd, 1)
                    if dch == 3 and e >= 1 and e + 1 < ne:
                        emit_ce(e + 1)

            if ne > 0:
                emit_ce(0)
                emit_gu_ocs(0, 0, 12)
            if ne > 1:
                emit_ce(1)
                emit_gu_ocs(1, 0, 2)
            tensor.wait_ge(s_gldg, 16 * 7)
            tensor.wait_ge(s_v1, V["cneg"])
            tensor.wait_ge(s_s1, S["hc3"])
            hmap = [0, 2, 1, 3]
            for dch in range(DC):
                W(tensor, s_yev, dch - 1)
                for th in range(2):
                    for bi in range(4):
                        ins = tensor.matmul(pd[dch % 2][:, th * 512:(th + 1) * 512],
                                            ctbuf[:, bi, dch * 128:(dch + 1) * 128],
                                            H_bf[:, hmap[bi], th * 512:(th + 1) * 512],
                                            start=(bi == 0), stop=(bi == 3))
                ins.then_inc(s_t1, 1)
            if ne > 0:
                tensor.wait_ge(s_yev, 16)
                emit_dn(0)
                for e in range(1, ne):
                    emit_gu_ocs(e, 2 if e == 1 else 0, 12)
                    emit_dn(e)


        @blk.scalar
        def _(scalar):
            def ds():
                scalar.drain()
            scalar.wait_ge(s_misc, 16 * 9)
            scalar.activation(aparm[:, 0:2], nu2[:], ACT.Exp)
            ds()
            scalar.activation(aparm[:, 2:4], aparm[:, 0:2], ACT.Exp, scale=-1.0, bias=-1.0)
            scalar.activation(aparm[:, 4:6], th2[:], ACT.Exp)
            scalar.drain().then_inc(s_s1, 1)
            for i in range(8):
                W(scalar, s_tok[i % 2], 16 * (i // 2 + 1))
                if i >= 2:
                    W(scalar, s_v1, V[f"stat{i - 2}"])
                scalar.activation(sq2[i % 2], tokstage[i % 2], ACT.Square).then_inc(s_s1, 1)
            scalar.wait_ge(s_v1, V["stat0"])
            scalar.activation(aparm[:, 20:22], aparm[:, 18:20], ACT.Ln)
            ds()
            scalar.activation(aparm[:, 16:18], aparm[:, 20:22], ACT.Exp, scale=0.5)
            scalar.drain().then_inc(s_s1, 1)
            scalar.wait_ge(s_v1, V["var"])
            scalar.activation(mq[:, 32:40], mq[:, 32:40], ACT.Ln, bias=consts[:, 1:2])
            ds()
            scalar.activation(mq[:, 24:32], mq[:, 32:40], ACT.Exp, scale=-0.5)
            scalar.drain().then_inc(s_s1, 1)
            for k in range(16):
                scalar.wait_ge(s_v1, V[f"xn{k}"])
                scalar.activation(xn_bf[:, k, :], xn32[k % 2], ACT.Copy).then_inc(s_s1, 1)
            scalar.wait_ge(s_v1, V["ltok"])
            for tc in range(8):
                scalar.activation(gat2[:, tc, :], gat[:, tc, :], ACT.Exp,
                                  bias=gsml[:, 8 + tc:9 + tc])
            scalar.drain().then_inc(s_s1, 1)
            scalar.wait_ge(s_v1, V["ssum8"])
            scalar.activation(gsml[:, 24:32], gsml[:, 16:24], ACT.Ln)
            ds()
            scalar.activation(gsml[:, 24:32], gsml[:, 24:32], ACT.Exp, scale=-1.0)
            scalar.drain().then_inc(s_s1, 1)
            scalar.wait_ge(s_v1, V["den"])
            scalar.activation(gsml[:, 40:48], gsml[:, 40:48], ACT.Ln)
            ds()
            scalar.activation(gsml[:, 40:48], gsml[:, 40:48], ACT.Exp, scale=-1.0)
            scalar.drain().then_inc(s_s1, 1)
            scalar.activation(aparm[:, 8:10], aparm[:, 6:8], ACT.Sin)
            scalar.activation(aparm[:, 10:12], aparm[:, 22:24], ACT.Sin)
            scalar.drain().then_inc(s_s1, 1)
            for j in range(6 if ne > 0 else 0):
                scalar.wait_ge(s_gud, 2 + j)
                scalar.activation(gsil[:, j, :], pg[j % 2][:, :], ACT.Silu).then_inc(s_sil, 1)
            scalar.wait_ge(s_v1, V["hfin"])
            scalar.activation(H_bf[:, 0, :], png_re[0][:, :], ACT.Copy).then_inc(s_s1, 1)
            scalar.activation(H_bf[:, 1, :], png_im[0][:, :], ACT.Copy).then_inc(s_s1, 1)
            scalar.activation(H_bf[:, 2, :], png_re[1][:, :], ACT.Copy).then_inc(s_s1, 1)
            scalar.activation(H_bf[:, 3, :], png_im[1][:, :], ACT.Copy)
            scalar.drain().then_inc(s_s1, 1)
            for e in range(1, ne):
                for j in range(6):
                    scalar.wait_ge(s_gud, 13 * e + 2 + j)
                    scalar.activation(gsil[:, j, :], pg[j % 2][:, :], ACT.Silu).then_inc(s_sil, 1)

        @blk.vector
        def _(vector):
            def dv():
                vector.drain()
            vector.wait_ge(s_s1, S["aexp"])
            vector.tensor_scalar(aparm[:, 6:8], aparm[:, 4:6], 1.0 / TWO_PI, MAGIC, ALU.mult, ALU.add)
            dv()
            vector.tensor_scalar(aparm[:, 6:8], aparm[:, 6:8], -MAGIC, None, ALU.add)
            dv()
            vector.scalar_tensor_tensor(aparm[:, 6:8], aparm[:, 6:8], -TWO_PI, aparm[:, 4:6],
                                        ALU.mult, ALU.add)
            dv()
            vector.tensor_scalar(aparm[:, 22:24], aparm[:, 6:8], TWO_PI / 4, None, ALU.add)
            dv()
            vector.tensor_scalar(aparm[:, 20:22], aparm[:, 22:24], TWO_PI / 2, None, ALU.is_gt)
            dv()
            vector.scalar_tensor_tensor(aparm[:, 22:24], aparm[:, 20:22], -TWO_PI, aparm[:, 22:24],
                                        ALU.mult, ALU.add)
            dv()
            vector.tensor_tensor(aparm[:, 18:20], aparm[:, 2:4], aparm[:, 2:4], ALU.mult)
            dv()
            vector.tensor_scalar(aparm[:, 18:20], aparm[:, 18:20], -1.0, 1.0, ALU.mult, ALU.add)
            dv()
            for i in range(8):
                W(vector, s_tok[i % 2], 16 * (i // 2 + 1))
                W(vector, s_s1, S[f"sq{i}"])
                vector.reduce_sum(mq[:, 8 + i:9 + i], sq2[i % 2], AX.X)
                vector.reduce_sum(mq[:, i:i + 1], tokstage[i % 2], AX.X).then_inc(s_v1, 1)
            dv()
            vector.tensor_scalar(mq[:, 16:24], mq[:, 0:8], 1.0 / D, None, ALU.mult)
            vector.tensor_scalar(mq[:, 32:40], mq[:, 8:16], 1.0 / D, None, ALU.mult)
            dv()
            vector.tensor_tensor(mq[:, 0:8], mq[:, 16:24], mq[:, 16:24], ALU.mult)
            dv()
            vector.tensor_tensor(mq[:, 32:40], mq[:, 32:40], mq[:, 0:8], ALU.subtract)
            vector.drain().then_inc(s_v1, 1)
            vector.tensor_copy(mr16[:, 0:8], mq[:, 16:24])
            vector.wait_ge(s_s1, S["r8"])
            vector.tensor_copy(mr16[:, 8:16], mq[:, 24:32])
            vector.drain().then_inc(s_v1, 1)
            vector.wait_ge(s_t1, T1["mrT"])
            vector.tensor_copy(mr_sb[:], pA[0:16, 0:128])
            vector.drain().then_inc(s_v1, 1)
            vector.wait_ge(s_t1, T1["rbc"])
            vector.tensor_copy(m_b[:, :], pA[:])
            vector.tensor_copy(r_b[:, :], pB[:])
            vector.drain().then_inc(s_v1, 1)
            for k in range(16):
                W(vector, s_xc[k % 2], 16 * (k // 2 + 1))
                if k >= 2:
                    vector.wait_ge(s_s1, S[f"cast{k - 2}"])
                vector.tensor_tensor(xn32[k % 2], xstage[k % 2], m_b[:, :], ALU.subtract)
                vector.tensor_tensor(xn32[k % 2], xn32[k % 2], r_b[:, :], ALU.mult)
                vector.scalar_tensor_tensor(out_acc[:, k, :], xn32[k % 2], d_col[:, k:k + 1],
                                            xstage[k % 2], ALU.mult, ALU.add)
                vector.drain().then_inc(s_v1, 1)
            vector.wait_ge(s_t1, T1["lg"])
            vector.tensor_copy(lgT_sb[:], pC[0:32, :])
            vector.drain().then_inc(s_v1, 1)
            vector.wait_ge(s_t1, T1["bu_re0"])
            vector.tensor_scalar(bufA_re[:, :], pg[0][:], aparm[:, 16 + 0:17 + 0],
                                 None, ALU.mult).then_inc(s_v1, 1)
            vector.wait_ge(s_t1, T1["bu_re1"])
            vector.tensor_scalar(bufB_re[:, :], pg[1][:], aparm[:, 16 + 1:17 + 1],
                                 None, ALU.mult).then_inc(s_v1, 1)
            vector.wait_ge(s_t1, T1["gt7"])
            for tc in range(8):
                vector.tensor_copy(gat[:, tc, :], pD[0:128, tc * 128:tc * 128 + 32])
            dv()
            vector.reduce_max(gsml[:, 0:8], gat[:], AX.X)
            dv()
            vector.tensor_scalar(gsml[:, 8:16], gsml[:, 0:8], -1.0, None, ALU.mult)
            vector.drain().then_inc(s_v1, 1)
            vector.wait_ge(s_s1, S["gexp"])
            vector.reduce_sum(gsml[:, 16:24], gat2[:], AX.X)
            vector.drain().then_inc(s_v1, 1)
            vector.wait_ge(s_s1, S["inv"])
            for tc in range(8):
                vector.tensor_scalar(gat2[:, tc, :], gat2[:, tc, :], gsml[:, 24 + tc:25 + tc],
                                     None, ALU.mult)
            dv()
            for tc in range(8):
                vector.max(tv[:, tc, :], gat2[:, tc, :])
            dv()
            vector.reduce_sum(gsml[:, 16:24], tv[:], AX.X)
            vector.tensor_copy(cumb[:], tv[:])
            dv()
            for j in range(1, 8):
                vector.tensor_tensor(cumb[:, :, j:j + 1], cumb[:, :, j - 1:j],
                                     tv[:, :, j:j + 1], ALU.add)
                dv()
            vector.tensor_scalar(gsml[:, 32:40], gsml[:, 16:24], 0.8, None, ALU.mult)
            dv()
            for tc in range(8):
                vector.tensor_scalar(cumb[:, tc, :], cumb[:, tc, :], gsml[:, 32 + tc:33 + tc],
                                     None, ALU.is_lt)
            dv()
            for tc in range(8):
                vector.tensor_copy(gat3[:, tc, 0:7], cumb[:, tc, 0:7])
                vector.tensor_scalar(gat3[:, tc, 7:8], gat2[:, tc, 0:1], 0.0, None, ALU.mult)
            dv()
            for tc in range(8):
                vector.tensor_copy(cumb[:, tc, 1:8], gat3[:, tc, 0:7])
                vector.tensor_scalar(cumb[:, tc, 0:1], gat2[:, tc, 0:1], 0.0, 1.0, ALU.mult, ALU.add)
            dv()
            vector.tensor_scalar(cumb[:].rearrange("p a b -> p (a b)"),
                                 cumb[:].rearrange("p a b -> p (a b)"),
                                 -3.0e38, 3.0e38, ALU.mult, ALU.add)
            dv()
            vector.tensor_tensor(cumb[:], cumb[:], tv[:], ALU.add)
            dv()
            vector.tensor_reduce(gsml[:, 32:40], cumb[:], AX.X, ALU.min)
            dv()
            for tc in range(8):
                vector.tensor_scalar(gat3[:, tc, :], gat2[:, tc, :], gsml[:, 32 + tc:33 + tc],
                                     None, ALU.is_ge)
            dv()
            vector.tensor_tensor(gat3[:], gat2[:], gat3[:], ALU.mult)
            dv()
            vector.reduce_sum(gsml[:, 40:48], gat3[:], AX.X)
            vector.drain().then_inc(s_v1, 1)
            vector.wait_ge(s_s1, S["dinv"])
            for tc in range(8):
                vector.tensor_scalar(gat2[:, tc, :], gat3[:, tc, :],
                                     gsml[:, 40 + tc:41 + tc], None, ALU.mult)
            vector.drain().then_inc(s_v1, 1)
            vector.wait_ge(s_t1, T1["ct7"])
            for tc in range(8):
                vector.tensor_copy(combT[:, tc * 128:(tc + 1) * 128],
                                   pD[0:32, tc * 128:(tc + 1) * 128])
            vector.drain().then_inc(s_v1, 1)
            vector.wait_ge(s_t1, T1["bu_im0"])
            vector.tensor_scalar(bufA_im[:, :], pg[0][:], aparm[:, 16 + 0:17 + 0],
                                 None, ALU.mult).then_inc(s_v1, 1)
            vector.wait_ge(s_t1, T1["bu_im1"])
            vector.tensor_scalar(bufB_im[:, :], pg[1][:], aparm[:, 16 + 1:17 + 1],
                                 None, ALU.mult).then_inc(s_v1, 1)
            vector.wait_ge(s_s1, S["sins"])
            vector.tensor_tensor(aparm[:, 12:14], aparm[:, 2:4], aparm[:, 10:12], ALU.mult)
            vector.tensor_tensor(aparm[:, 14:16], aparm[:, 2:4], aparm[:, 8:10], ALU.mult)
            dv()
            vector.tensor_copy(apow_re[:, 0, :], aparm[:, 12:14])
            vector.tensor_copy(apow_im[:, 0, :], aparm[:, 14:16])
            dv()
            for k in range(10):
                re_k = apow_re[:, k, :]; im_k = apow_im[:, k, :]
                vector.tensor_tensor(aparm[:, 20:22], re_k, re_k, ALU.mult)
                vector.tensor_tensor(aparm[:, 22:24], im_k, im_k, ALU.mult)
                dv()
                vector.tensor_tensor(apow_re[:, k + 1, :], aparm[:, 20:22], aparm[:, 22:24],
                                     ALU.subtract)
                vector.tensor_tensor(aparm[:, 20:22], re_k, im_k, ALU.mult)
                dv()
                vector.tensor_scalar(apow_im[:, k + 1, :], aparm[:, 20:22], 2.0, None, ALU.mult)
                dv()
            vector.tensor_scalar(napow_im[:].rearrange("p a b -> p (a b)"),
                                 apow_im[:].rearrange("p a b -> p (a b)"), -1.0, None, ALU.mult)
            vector.tensor_scalar(qp_re[:, 0, :], apow_re[:, 0, :], 0.0, 1.0, ALU.mult, ALU.add)
            vector.tensor_scalar(qp_im[:, 0, :], apow_re[:, 0, :], 0.0, None, ALU.mult)
            dv()
            for p in range(7):
                a10r = apow_re[:, 10, :]; a10i = apow_im[:, 10, :]
                vector.tensor_tensor(aparm[:, 20:22], qp_re[:, p, :], a10r, ALU.mult)
                vector.tensor_tensor(aparm[:, 22:24], qp_im[:, p, :], a10i, ALU.mult)
                dv()
                vector.tensor_tensor(qp_re[:, p + 1, :], aparm[:, 20:22], aparm[:, 22:24],
                                     ALU.subtract)
                vector.tensor_tensor(aparm[:, 20:22], qp_re[:, p, :], a10i, ALU.mult)
                dv()
                vector.tensor_tensor(aparm[:, 22:24], qp_im[:, p, :], a10r, ALU.mult)
                dv()
                vector.tensor_tensor(qp_im[:, p + 1, :], aparm[:, 20:22], aparm[:, 22:24],
                                     ALU.add)
                dv()
            vector.tensor_copy(qp_re[0:1, 0, 0:1], qp_re[0:1, 0, 0:1])
            vector.drain().then_inc(s_v1, 1)
            def e0_evacs(k):
                if ne == 0:
                    return
                sched = {0: [("ce", 0)], 1: [("g", 0)], 2: [("g", 1), ("g", 2)],
                         3: [("g", 3), ("g", 4)], 4: [("g", 5), ("h", 0)],
                         5: [("h", 1), ("h", 2), ("h", 3)],
                         6: [("h", 4), ("h", 5)]}
                for kind, j in sched.get(k, []):
                    if kind == "ce":
                        vector.wait_ge(s_gud, 1)
                        vector.tensor_copy(ce_b[0][:, :], pA[:]).then_inc(s_gue, 1)
                    elif kind == "g":
                        vector.wait_ge(s_sil, j + 1)
                        vector.tensor_tensor(gsil[:, j, :], gsil[:, j, :], ce_b[0][:, :],
                                             ALU.mult).then_inc(s_gue, 1)
                    else:
                        vector.wait_ge(s_gud, 8 + j)
                        vector.tensor_tensor(h_b[0][:, j, :], gsil[:, j, :], pg[j % 2][:],
                                             ALU.mult).then_inc(s_gue, 1)

            vector.wait_ge(s_t1, T1["bu_im1"])
            cur_re = [bufA_re, bufB_re]; cur_im = [bufA_im, bufB_im]
            nxt_re = [png_re[0], png_re[1]]; nxt_im = [png_im[0], png_im[1]]
            for k in range(7):
                e0_evacs(k)
                dstep = 1 << k
                n = T - dstep
                for ch in range(2):
                    sre = apow_re[:, k, ch:ch + 1]
                    sim_ = apow_im[:, k, ch:ch + 1]
                    nsim = napow_im[:, k, ch:ch + 1]
                    a_re, a_im = cur_re[ch], cur_im[ch]
                    b_re, b_im = nxt_re[ch], nxt_im[ch]
                    vector.scalar_tensor_tensor(b_re[:, dstep:T], a_re[:, 0:n], sre,
                                                a_re[:, dstep:T], ALU.mult, ALU.add)
                    vector.scalar_tensor_tensor(b_re[:, dstep:T], a_im[:, 0:n], nsim,
                                                b_re[:, dstep:T], ALU.mult, ALU.add)
                    vector.scalar_tensor_tensor(b_im[:, dstep:T], a_im[:, 0:n], sre,
                                                a_im[:, dstep:T], ALU.mult, ALU.add)
                    vector.scalar_tensor_tensor(b_im[:, dstep:T], a_re[:, 0:n], sim_,
                                                b_im[:, dstep:T], ALU.mult, ALU.add)
                    vector.tensor_copy(b_re[:, 0:dstep], a_re[:, 0:dstep])
                    vector.tensor_copy(b_im[:, 0:dstep], a_im[:, 0:dstep])
                    dv()
                cur_re, nxt_re = nxt_re, cur_re
                cur_im, nxt_im = nxt_im, cur_im
            for ch in range(2):
                vector.tensor_copy(f_sb[:, 2 * ch:2 * ch + 1], cur_re[ch][:, T - 1:T])
                vector.tensor_copy(f_sb[:, 2 * ch + 1:2 * ch + 2], cur_im[ch][:, T - 1:T])
            vector.drain().then_inc(s_v1, 1)
            vector.wait_ge(s_fal, 16)
            for p in range(7):
                for j in range(7):
                    vector.tensor_scalar(gtmp[:, p, :, j], f_all[:, j, :],
                                         selc[:, p * 8 + j:p * 8 + j + 1], None, ALU.mult)
            dv()
            vector.tensor_reduce(gacc[:, 0:7, :], gtmp[:], AX.X, ALU.add)
            vector.tensor_scalar(gacc[:, 7, :], f_all[:, 7, :], 0.0, None, ALU.mult)
            dv()
            for p in range(8):
                for ch in range(2):
                    gre = gacc[:, p, 2 * ch:2 * ch + 1]
                    gim = gacc[:, p, 2 * ch + 1:2 * ch + 2]
                    qre = qp_re[:, p, ch:ch + 1]
                    qim = qp_im[:, p, ch:ch + 1]
                    vector.tensor_scalar(qg[:, p, ch, 0:1], gre, qre, None, ALU.mult)
                    vector.tensor_scalar(qg[:, p, ch, 1:2], gim, qim, None, ALU.mult)
                    vector.tensor_scalar(qg[:, p, ch, 2:3], gim, qre, None, ALU.mult)
                    vector.tensor_scalar(qg[:, p, ch, 3:4], gre, qim, None, ALU.mult)
            dv()
            for ch in range(2):
                for t4 in range(4):
                    vector.tensor_reduce(hsum[:, ch, t4:t4 + 1],
                                         qg[:, :, ch, t4:t4 + 1].rearrange("p a b -> p (a b)"),
                                         AX.X, ALU.add)
            dv()
            for ch in range(2):
                vector.tensor_tensor(hin[:, 2 * ch:2 * ch + 1], hsum[:, ch, 0:1],
                                     hsum[:, ch, 1:2], ALU.subtract)
                vector.tensor_tensor(hin[:, 2 * ch + 1:2 * ch + 2], hsum[:, ch, 2:3],
                                     hsum[:, ch, 3:4], ALU.add)
            dv()
            if ne > 1:
                vector.wait_ge(s_gud, 14)
                vector.tensor_copy(ce_b[1][:, :], pA[:]).then_inc(s_gue, 1)
            corr_re = [bufA_re, bufB_re]; corr_im = [bufA_im, bufB_im]
            for ch in range(2):
                c_re, c_im = corr_re[ch], corr_im[ch]
                hre = hin[:, 2 * ch:2 * ch + 1]; him = hin[:, 2 * ch + 1:2 * ch + 2]
                are = apow_re[:, 0, ch:ch + 1]; aim = apow_im[:, 0, ch:ch + 1]
                vector.tensor_scalar(c_re[:, 0:1], hre, are, None, ALU.mult)
                vector.tensor_scalar(aparm[:, 20:21], him, aim, None, ALU.mult)
                dv()
                vector.tensor_tensor(c_re[:, 0:1], c_re[:, 0:1], aparm[:, 20:21], ALU.subtract)
                vector.tensor_scalar(c_im[:, 0:1], him, are, None, ALU.mult)
                vector.tensor_scalar(aparm[:, 20:21], hre, aim, None, ALU.mult)
                dv()
                vector.tensor_tensor(c_im[:, 0:1], c_im[:, 0:1], aparm[:, 20:21], ALU.add)
                dv()
                for k in range(7):
                    dstep = 1 << k
                    hi = min(2 * dstep, T)
                    n = hi - dstep
                    sre = apow_re[:, k, ch:ch + 1]
                    sim_ = apow_im[:, k, ch:ch + 1]
                    nsim = napow_im[:, k, ch:ch + 1]
                    vector.tensor_scalar(c_re[:, dstep:hi], c_re[:, 0:n], sre, None, ALU.mult)
                    dv()
                    vector.scalar_tensor_tensor(c_re[:, dstep:hi], c_im[:, 0:n], nsim,
                                                c_re[:, dstep:hi], ALU.mult, ALU.add)
                    vector.tensor_scalar(c_im[:, dstep:hi], c_im[:, 0:n], sre, None, ALU.mult)
                    dv()
                    vector.scalar_tensor_tensor(c_im[:, dstep:hi], c_re[:, 0:n], sim_,
                                                c_im[:, dstep:hi], ALU.mult, ALU.add)
                    dv()
                vector.tensor_tensor(cur_re[ch][:, 0:128], cur_re[ch][:, 0:128],
                                     c_re[:, 0:128], ALU.add)
                vector.tensor_tensor(cur_im[ch][:, 0:128], cur_im[ch][:, 0:128],
                                     c_im[:, 0:128], ALU.add)
            vector.drain().then_inc(s_v1, 1)
            vector.wait_ge(s_gldg, 16 * 7)
            vector.tensor_scalar(ctbuf[:, 2, :], ctbuf[:, 2, :], -1.0, None, ALU.mult)
            vector.tensor_scalar(ctbuf[:, 3, :], ctbuf[:, 3, :], -1.0, None, ALU.mult)
            vector.drain().then_inc(s_v1, 1)
            for dch in range(DC):
                vector.wait_ge(s_t1, T1[f"y{dch}"])
                vector.tensor_tensor(out_acc[:, dch, :], out_acc[:, dch, :],
                                     pd[dch % 2][:], ALU.add).then_inc(s_yev, 1)
            def emit_ce_evac(e):
                vector.wait_ge(s_gud, 13 * e + 1)
                vector.tensor_copy(ce_b[e % 2][:, :], pA[:]).then_inc(s_gue, 1)

            for e in range(ne):
                if e >= 1:
                    for j in range(6):
                        vector.wait_ge(s_sil, 6 * e + j + 1)
                        vector.tensor_tensor(gsil[:, j, :], gsil[:, j, :], ce_b[e % 2][:, :],
                                             ALU.mult).then_inc(s_gue, 1)
                    for j in range(6):
                        if j == 0:
                            W(vector, s_dnd, 16 * (e - 1))
                        vector.wait_ge(s_gud, 13 * e + 8 + j)
                        vector.tensor_tensor(h_b[e % 2][:, j, :], gsil[:, j, :], pg[j % 2][:],
                                             ALU.mult).then_inc(s_gue, 1)
                for dch in range(16):
                    vector.wait_ge(s_dnd, 16 * e + dch + 1)
                    vector.tensor_tensor(out_acc[:, dch, :], out_acc[:, dch, :],
                                         pd[dch % 2][:], ALU.add).then_inc(s_dne, 1)
                    if dch == 3 and e >= 1 and e + 1 < ne:
                        emit_ce_evac(e + 1)

    return nc


_NC_CACHE = None


def _get_nc():
    global _NC_CACHE
    if _NC_CACHE is None:
        _NC_CACHE = build_nc()
    return _NC_CACHE


def kernel(L0_attn_x, L0_attn_nu_log, L0_attn_theta_log, L0_attn_B_re, L0_attn_B_im,
           L0_attn_C_re, L0_attn_C_im, L0_attn_D, L0_ffn_gate_weight,
           L0_ffn_gate_up_proj, L0_ffn_down_proj):
    C = np.ascontiguousarray
    x = np.asarray(L0_attn_x, np.float32).reshape(8192, 2048)
    xT_full = C(x.T)
    nu_t = C(np.asarray(L0_attn_nu_log, np.float32).reshape(2, 128).T)
    theta_t = C(np.asarray(L0_attn_theta_log, np.float32).reshape(2, 128).T)
    d_t = C(np.asarray(L0_attn_D, np.float32).reshape(16, 128).T)
    bre_t = C(np.asarray(L0_attn_B_re, np.float32).T)
    bim_t = C(np.asarray(L0_attn_B_im, np.float32).T)
    cre_t = C(np.asarray(L0_attn_C_re, np.float32).T)
    cim_t = C(np.asarray(L0_attn_C_im, np.float32).T)
    gw_t = C(np.asarray(L0_ffn_gate_weight, np.float32).T)
    gup = np.asarray(L0_ffn_gate_up_proj, np.float32)
    gu_w = C(gup.transpose(0, 2, 1).reshape(NE, D, 6, 256).transpose(0, 2, 1, 3))
    dnp = np.asarray(L0_ffn_down_proj, np.float32)
    dn_w = C(dnp.transpose(0, 2, 1).reshape(NE, 768, 4, 512).transpose(0, 2, 1, 3))
    idm_bf = np.eye(128, dtype=ml_dtypes.bfloat16)
    idm_f32 = np.eye(128, dtype=np.float32)
    ones_bf = np.ones((1, 128), dtype=ml_dtypes.bfloat16)
    ones_f32_np = np.ones((1, 128), dtype=np.float32)
    consts_np = np.tile(np.array([[-1.0, 1e-5]], np.float32), (128, 1))

    in_maps = []
    for c in range(NCORES):
        selm = np.zeros((8, 8), np.float32)
        for j in range(c):
            selm[c - 1 - j, j] = 1.0
        selc_np = np.broadcast_to(selm.reshape(1, 64), (128, 64)).copy()
        in_maps.append(dict(
            xT=C(xT_full[:, c * T:(c + 1) * T]),
            xtok=C(x[c * T:(c + 1) * T, :]),
            nu_t=nu_t, theta_t=theta_t, d_t=d_t, selc_t=selc_np,
            bre_t=bre_t, bim_t=bim_t, cre_t=cre_t, cim_t=cim_t,
            gw_t=gw_t, gu_w=gu_w, dn_w=dn_w,
            idm_bf=idm_bf, idm_f32=idm_f32, ones_bf=ones_bf,
            ones_f32=ones_f32_np, consts_d=consts_np,
        ))
    nc = _get_nc()
    res = run_bass_kernel_spmd(nc, in_maps, core_ids=list(range(NCORES)))
    outT = np.concatenate([np.asarray(r["oT"]) for r in res.results], axis=1)
    return C(outT.T).reshape(1, 8192, 2048)


# version 09
# speedup vs baseline: 1.0162x; 1.0161x over previous
import numpy as np
import ml_dtypes
import concourse.bass as bass
import concourse.mybir as mybir
from concourse.bass_utils import run_bass_kernel_spmd

F32 = mybir.dt.float32
BF16 = mybir.dt.bfloat16
U8 = mybir.dt.uint8
ACT = mybir.ActivationFunctionType
ALU = mybir.AluOpType
AX = mybir.AxisListType

NCORES = 8
T = 1024
D = 2048
DC = 16
NE = 32
TWO_PI = 6.283185307179586
MAGIC = 8388608.0

def _marks(names):
    return {n: i + 1 for i, n in enumerate(names)}

V = _marks(
    [f"stat{i}" for i in range(8)] + ["var", "mr16", "mrsb", "mbrb"]
    + [f"xn{k}" for k in range(16)]
    + ["lgT", "bu_re0", "bu_re1", "ltok", "ssum8", "den", "comb", "combT",
       "bu_im0", "bu_im1", "tables", "f", "hfin", "cneg"]
)
S = _marks(
    ["aexp"] + [f"sq{i}" for i in range(8)] + ["gamma", "r8"]
    + [f"cast{k}" for k in range(16)]
    + ["gexp", "inv", "dinv", "sins", "hc0", "hc1", "hc2", "hc3"]
)
T1 = _marks(
    ["mrT", "mbc", "rbc", "lg"]
    + ["bu_re0", "bu_re1", "bu_im0", "bu_im1"]
    + [f"gt{t}" for t in range(8)]
    + [f"ct{t}" for t in range(8)]
    + [f"y{d}" for d in range(16)]
)


def W(eng, sem, n):
    if n > 0:
        eng.wait_ge(sem, n)


def build_nc(ne=NE, stage=None):
    nc = bass.Bass("TRN2", target_bir_lowering=False, detect_race_conditions=False)

    xT = nc.dram_tensor("xT", [D, T], F32, kind="ExternalInput")
    xtok = nc.dram_tensor("xtok", [T, D], F32, kind="ExternalInput")
    nu_t = nc.dram_tensor("nu_t", [128, 2], F32, kind="ExternalInput")
    theta_t = nc.dram_tensor("theta_t", [128, 2], F32, kind="ExternalInput")
    d_t = nc.dram_tensor("d_t", [128, 16], F32, kind="ExternalInput")
    selc_t = nc.dram_tensor("selc_t", [128, 64], F32, kind="ExternalInput")
    bre_t = nc.dram_tensor("bre_t", [D, 256], F32, kind="ExternalInput")
    bim_t = nc.dram_tensor("bim_t", [D, 256], F32, kind="ExternalInput")
    cre_t = nc.dram_tensor("cre_t", [256, D], F32, kind="ExternalInput")
    cim_t = nc.dram_tensor("cim_t", [256, D], F32, kind="ExternalInput")
    gw_t = nc.dram_tensor("gw_t", [D, NE], F32, kind="ExternalInput")
    gu_w = nc.dram_tensor("gu_w", [NE, 6, D, 256], F32, kind="ExternalInput")
    dn_w = nc.dram_tensor("dn_w", [NE, 4, 768, 512], F32, kind="ExternalInput")
    idm_bf_d = nc.dram_tensor("idm_bf", [128, 128], BF16, kind="ExternalInput")
    idm_f32_d = nc.dram_tensor("idm_f32", [128, 128], F32, kind="ExternalInput")
    ones_bf_d = nc.dram_tensor("ones_bf", [1, 128], BF16, kind="ExternalInput")
    ones_f32_d = nc.dram_tensor("ones_f32", [1, 128], F32, kind="ExternalInput")
    consts_d = nc.dram_tensor("consts_d", [128, 2], F32, kind="ExternalInput")
    oT = nc.dram_tensor("oT", [D, T], F32, kind="ExternalOutput")
    cc_in = nc.dram_tensor("cc_in", [128, 4], F32, kind="Internal")
    cc_out = nc.dram_tensor("cc_out", [128 * NCORES, 4], F32, kind="Internal")

    from contextlib import ExitStack
    ctx = ExitStack()
    with ctx:
        E = ctx.enter_context
        out_acc = E(nc.sbuf_tensor("out_acc", [128, DC, T], F32))
        xn_bf = E(nc.sbuf_tensor("xn_bf", [128, DC, T], BF16))
        gu_b = [E(nc.sbuf_tensor(f"gu_b{i}", [128, 16, 256], BF16)) for i in range(2)]
        dn_b = [E(nc.sbuf_tensor(f"dn_b{i}", [128, 6, 512], BF16)) for i in range(2)]
        gw_bf = E(nc.sbuf_tensor("gw_bf", [128, 16, NE], BF16))
        idm_bf = E(nc.sbuf_tensor("idm_bf_s", [128, 128], BF16))
        idm_f32 = E(nc.sbuf_tensor("idm_f32_s", [128, 128], F32))
        ones_bf = E(nc.sbuf_tensor("ones_bf_s", [1, 128], BF16))
        ones_f32 = E(nc.sbuf_tensor("ones_f32_s", [1, 128], F32))
        nu2 = E(nc.sbuf_tensor("nu2", [128, 2], F32))
        th2 = E(nc.sbuf_tensor("th2", [128, 2], F32))
        d_col = E(nc.sbuf_tensor("d_col", [128, 16], F32))
        selc = E(nc.sbuf_tensor("selc", [128, 64], F32))
        consts = E(nc.sbuf_tensor("consts", [128, 2], F32))
        aparm = E(nc.sbuf_tensor("aparm", [128, 24], F32))
        apow_re = E(nc.sbuf_tensor("apow_re", [128, 11, 2], F32))
        apow_im = E(nc.sbuf_tensor("apow_im", [128, 11, 2], F32))
        napow_im = E(nc.sbuf_tensor("napow_im", [128, 11, 2], F32))
        qp_re = E(nc.sbuf_tensor("qp_re", [128, 8, 2], F32))
        qp_im = E(nc.sbuf_tensor("qp_im", [128, 8, 2], F32))
        f_sb = E(nc.sbuf_tensor("f_sb", [128, 4], F32))
        f_all = E(nc.sbuf_tensor("f_all", [128, 8, 4], F32))
        gacc = E(nc.sbuf_tensor("gacc", [128, 8, 4], F32))
        gtmp = E(nc.sbuf_tensor("gtmp", [128, 7, 4, 7], F32))
        qg = E(nc.sbuf_tensor("qg", [128, 8, 2, 4], F32))
        hsum = E(nc.sbuf_tensor("hsum", [128, 2, 4], F32))
        hin = E(nc.sbuf_tensor("hin", [128, 4], F32))
        ce_row = [E(nc.sbuf_tensor(f"ce_row{i}", [1, T], BF16)) for i in range(2)]
        mq = E(nc.sbuf_tensor("mq", [128, 40], F32))
        mr16 = E(nc.sbuf_tensor("mr16", [128, 16], F32))
        mr_sb = E(nc.sbuf_tensor("mr_sb", [16, 128], F32))
        gat = E(nc.sbuf_tensor("gat", [128, 8, 32], F32))
        gat2 = E(nc.sbuf_tensor("gat2", [128, 8, 32], F32))
        gat3 = E(nc.sbuf_tensor("gat3", [128, 8, 32], F32))
        tv = E(nc.sbuf_tensor("tv", [128, 8, 8], F32))
        cumb = E(nc.sbuf_tensor("cumb", [128, 8, 8], F32))
        gsml = E(nc.sbuf_tensor("gsml", [128, 48], F32))
        combT = E(nc.sbuf_tensor("combT", [32, T], BF16))
        h0_p = E(nc.sbuf_tensor("h0_p", [128, 6, T], BF16))
        gsil_p = E(nc.sbuf_tensor("gsil_p", [128, 6, T], BF16))
        ceb0_p = E(nc.sbuf_tensor("ceb0_p", [128, T], BF16))

        arena = E(nc.sbuf_tensor("arena", [128, 40960], U8))

        def afr(byte0, nbytes, dt=F32):
            return arena[:, byte0:byte0 + nbytes].bitcast(dt)

        tokstage = [afr(0, 8192), afr(8192, 8192)]
        sq2 = [afr(16384, 8192), afr(24576, 8192)]
        xstage = [afr(0, 4096), afr(4096, 4096)]
        xn32 = [afr(8192, 4096), afr(12288, 4096)]
        m_b = afr(16384, 4096)
        r_b = afr(20480, 4096)
        bt_re = afr(0, 8192, BF16).rearrange("p (c n) -> p c n", c=16)
        bt_im = afr(8192, 8192, BF16).rearrange("p (c n) -> p c n", c=16)
        bufA_re = afr(16384, 4096)
        bufA_im = afr(20480, 4096)
        bufB_re = afr(24576, 4096)
        bufB_im = afr(28672, 4096)
        lgT_sb = afr(32768, 4096)[0:32, :]
        png_re = [afr(0, 4096), afr(8192, 4096)]
        png_im = [afr(4096, 4096), afr(12288, 4096)]
        H_bf = afr(32768, 8192, BF16).rearrange("p (b t) -> p b t", b=4)
        ctbuf = afr(0, 16384, BF16).rearrange("p (b t) -> p b t", b=4)
        h_b = [h0_p[:], afr(12288, 12288, BF16).rearrange("p (j t) -> p j t", j=6)]
        gsil = gsil_p[:]
        ce_b = [ceb0_p[:], afr(18432, 2048, BF16)]

        pA = E(nc.psum_tensor("pA", [128, 1024], F32))
        pB = E(nc.psum_tensor("pB", [128, 1024], F32))
        pC = E(nc.psum_tensor("pC", [128, 1024], F32))
        pD = E(nc.psum_tensor("pD", [128, 1024], F32))
        pg = [pA, pB]
        pd = [pC, pD]

        s_misc = E(nc.semaphore("s_misc"))
        s_gldg = E(nc.semaphore("s_gldg"))
        s_tok = [E(nc.semaphore(f"s_tok{i}")) for i in range(2)]
        s_xc = [E(nc.semaphore(f"s_xc{i}")) for i in range(2)]
        s_w = [E(nc.semaphore(f"s_w{i}")) for i in range(2)]
        s_d = [E(nc.semaphore(f"s_d{i}")) for i in range(2)]
        s_ce = [E(nc.semaphore(f"s_ce{i}")) for i in range(2)]
        s_mrow = E(nc.semaphore("s_mrow"))
        s_ccs = E(nc.semaphore("s_ccs"))
        s_fal = E(nc.semaphore("s_fal"))
        s_v1 = E(nc.semaphore("s_v1"))
        s_s1 = E(nc.semaphore("s_s1"))
        s_t1 = E(nc.semaphore("s_t1"))
        s_gud = E(nc.semaphore("s_gud"))
        s_gue = E(nc.semaphore("s_gue"))
        s_sil = E(nc.semaphore("s_sil"))
        s_dnd = E(nc.semaphore("s_dnd"))
        s_dne = E(nc.semaphore("s_dne"))
        s_yev = E(nc.semaphore("s_yev"))
        s_st = E(nc.semaphore("s_st"))

        nc.const_aps.register(F32, -1.0, consts[:, 0:1])

        blk = E(nc.Block())

        @blk.sync
        def _(sync):
            sync.dma_start(nu2[:], nu_t[:, :]).then_inc(s_misc, 16)
            sync.dma_start(th2[:], theta_t[:, :]).then_inc(s_misc, 16)
            sync.dma_start(d_col[:], d_t[:, :]).then_inc(s_misc, 16)
            sync.dma_start(selc[:], selc_t[:, :]).then_inc(s_misc, 16)
            sync.dma_start(idm_bf[:], idm_bf_d[:, :]).then_inc(s_misc, 16)
            sync.dma_start(idm_f32[:], idm_f32_d[:, :]).then_inc(s_misc, 16)
            sync.dma_start(ones_bf[:], ones_bf_d[:, :]).then_inc(s_misc, 16)
            sync.dma_start(ones_f32[:], ones_f32_d[:, :]).then_inc(s_misc, 16)
            sync.dma_start(consts[:], consts_d[:, :]).then_inc(s_misc, 16)
            for i in range(8):
                if i >= 2:
                    sync.wait_ge(s_v1, V[f"stat{i - 2}"])
                sync.dma_start(tokstage[i % 2], xtok[i * 128:(i + 1) * 128, :]).then_inc(s_tok[i % 2], 16)
            sync.wait_ge(s_v1, V["mrsb"])
            sync.dma_start(m_b[0:1, 0:1024], mr_sb[0:8, 0:128]).then_inc(s_mrow, 16)
            sync.dma_start(r_b[0:1, 0:1024], mr_sb[8:16, 0:128]).then_inc(s_mrow, 16)
            for k in range(16):
                if k < 2:
                    sync.wait_ge(s_v1, V["mbrb"])
                else:
                    sync.wait_ge(s_v1, V[f"xn{k - 2}"])
                sync.dma_start(xstage[k % 2], xT[k * 128:(k + 1) * 128, :]).then_inc(s_xc[k % 2], 16)
            for e in range(min(2, ne)):
                sync.wait_ge(s_v1, V["combT"])
                sync.dma_start(ce_row[e % 2][:], combT[e:e + 1, :]).then_inc(s_ce[e % 2], 16)
            sync.wait_ge(s_v1, V["f"])
            sync.dma_start(cc_in[:, :], f_sb[:]).then_inc(s_ccs, 16)
            for e in range(2, ne):
                sync.wait_ge(s_gud, 13 * (e - 2) + 1)
                sync.dma_start(ce_row[e % 2][:], combT[e:e + 1, :]).then_inc(s_ce[e % 2], 16)
            if stage == "xn":
                sync.wait_ge(s_v1, V["xn15"])
            elif stage == "hfin":
                sync.wait_ge(s_v1, V["hfin"])
            elif ne > 0:
                sync.wait_ge(s_dne, 16 * ne)
            else:
                sync.wait_ge(s_yev, 16)
            sync.dma_start(oT[:, :].rearrange("(c p) t -> p c t", p=128), out_acc[:]).then_inc(s_st, 16)
            sync.wait_ge(s_st, 16)

        @blk.gpsimd
        def _(gpsimd):
            gpsimd.dma_start(gw_bf[:], gw_t[:, :].rearrange("(c p) e -> p c e", p=128)).then_inc(s_gldg, 16)
            gpsimd.wait_ge(s_v1, V["xn15"])
            gpsimd.wait_ge(s_s1, S["cast15"])
            gpsimd.dma_start(bt_re, bre_t[:, :].rearrange("(c p) n -> p c n", p=128)).then_inc(s_gldg, 16)
            gpsimd.dma_start(bt_im, bim_t[:, :].rearrange("(c p) n -> p c n", p=128)).then_inc(s_gldg, 16)
            def gu_load(g6):
                if g6 >= 2:
                    ep, sp = divmod(g6 - 2, 6)
                    gpsimd.wait_ge(s_gud, 13 * ep + 3 + 2 * sp)
                gpsimd.dma_start(gu_b[g6 % 2][:],
                                 gu_w[g6 // 6, g6 % 6].rearrange("(c p) o -> p c o", p=128)).then_inc(s_w[g6 % 2], 16)

            def dn_load(q):
                if q >= 2:
                    eq, qq = divmod(q - 2, 4)
                    gpsimd.wait_ge(s_dnd, 16 * eq + 4 * qq + 4)
                gpsimd.dma_start(dn_b[q % 2][:],
                                 dn_w[q // 4, q % 4].rearrange("(c p) d -> p c d", p=128)).then_inc(s_d[q % 2], 16)

            for g6 in range(min(3, 6 * ne)):
                gu_load(g6)
            for q in range(min(2, 4 * ne)):
                dn_load(q)
            gpsimd.wait_ge(s_ccs, 16)
            gpsimd.collective_compute(
                "AllGather", ALU.bypass,
                ins=[cc_in[:, :]], outs=[cc_out[:, :]],
                replica_groups=[list(range(NCORES))],
            ).then_inc(s_ccs, 1)
            gpsimd.wait_ge(s_ccs, 17)
            gpsimd.dma_start(f_all[:], cc_out[:, :].rearrange("(r p) c -> p r c", p=128)).then_inc(s_fal, 16)
            for g6 in range(3, min(7, 6 * ne)):
                gu_load(g6)
            gpsimd.wait_ge(s_s1, S["hc3"])
            for b, (src, c0) in enumerate([(cre_t, 0), (cre_t, 128), (cim_t, 0), (cim_t, 128)]):
                gpsimd.dma_start(ctbuf[:, b, :], src[c0:c0 + 128, :]).then_inc(s_gldg, 16)
            if 6 * ne > 7:
                gu_load(7)
            for q in range(2, min(4, 4 * ne)):
                dn_load(q)
            for e in range(ne):
                for s6 in range(6):
                    g6 = 6 * e + s6
                    if g6 < 8:
                        continue
                    gu_load(g6)
                for qi in range(4):
                    q = 4 * e + qi
                    if q < 4:
                        continue
                    dn_load(q)

        @blk.tensor
        def _(tensor):
            tensor.wait_ge(s_misc, 16 * 9)
            tensor.wait_ge(s_v1, V["mr16"])
            tensor.transpose(pA[0:16, 0:128], mr16[:], idm_f32[:]).then_inc(s_t1, 1)
            tensor.wait_ge(s_mrow, 16 * 2)
            for th in range(2):
                ins = tensor.matmul(pA[:, th * 512:(th + 1) * 512], ones_f32[:],
                                    m_b[0:1, th * 512:(th + 1) * 512], start=True, stop=True)
            ins.then_inc(s_t1, 1)
            for th in range(2):
                ins = tensor.matmul(pB[:, th * 512:(th + 1) * 512], ones_f32[:],
                                    r_b[0:1, th * 512:(th + 1) * 512], start=True, stop=True)
            ins.then_inc(s_t1, 1)
            tensor.wait_ge(s_s1, S["cast15"])
            tensor.wait_ge(s_gldg, 16 * 1)
            for th in range(2):
                for k in range(DC):
                    ins = tensor.matmul(pC[0:32, th * 512:(th + 1) * 512], gw_bf[:, k, :],
                                        xn_bf[:, k, th * 512:(th + 1) * 512],
                                        start=(k == 0), stop=(k == DC - 1))
            ins.then_inc(s_t1, 1)
            tensor.wait_ge(s_gldg, 16 * 3)
            for comp in range(2):
                bt = bt_re if comp == 0 else bt_im
                if comp == 1:
                    tensor.wait_ge(s_v1, V["bu_re1"])
                for ch in range(2):
                    for th in range(2):
                        for k in range(DC):
                            ins = tensor.matmul(pg[ch][:, th * 512:(th + 1) * 512],
                                                bt[:, k, ch * 128:(ch + 1) * 128],
                                                xn_bf[:, k, th * 512:(th + 1) * 512],
                                                start=(k == 0), stop=(k == DC - 1))
                    ins.then_inc(s_t1, 1)
            tensor.wait_ge(s_v1, V["lgT"])
            for tc in range(8):
                tensor.transpose(pD[0:128, tc * 128:tc * 128 + 32],
                                 lgT_sb[:, tc * 128:(tc + 1) * 128],
                                 idm_f32[0:32, 0:32]).then_inc(s_t1, 1)
            tensor.wait_ge(s_v1, V["comb"])
            for tc in range(8):
                tensor.transpose(pD[0:32, tc * 128:(tc + 1) * 128],
                                 gat2[:, tc, :], idm_f32[:]).then_inc(s_t1, 1)
            def emit_ce(e):
                W(tensor, s_ce[e % 2], 16 * (e // 2 + 1))
                if e == 0:
                    tensor.wait_ge(s_v1, V["bu_im0"])
                W(tensor, s_gue, 13 * e - 1)
                for th in range(2):
                    ins = tensor.matmul(pA[:, th * 512:(th + 1) * 512], ones_bf[:],
                                        ce_row[e % 2][0:1, th * 512:(th + 1) * 512],
                                        start=True, stop=True)
                ins.then_inc(s_gud, 1)

            def emit_gu_ocs(e, k_lo, k_hi):
                for K in range(k_lo, k_hi):
                    g6 = 6 * e + K // 2
                    if K % 2 == 0:
                        W(tensor, s_w[g6 % 2], 16 * (g6 // 2 + 1))
                    if e == 0 and K == 1:
                        tensor.wait_ge(s_v1, V["bu_im1"])
                    if e == 1 and 2 <= K <= 7:
                        tensor.wait_ge(s_sil, 5 + K)
                    else:
                        W(tensor, s_gue, (13 * e + 1) if K == 0 else (13 * e + K))
                    for th in range(2):
                        for k in range(DC):
                            ins = tensor.matmul(pg[K % 2][:, th * 512:(th + 1) * 512],
                                                gu_b[g6 % 2][:, k, (K % 2) * 128:(K % 2 + 1) * 128],
                                                xn_bf[:, k, th * 512:(th + 1) * 512],
                                                start=(k == 0), stop=(k == DC - 1))
                    ins.then_inc(s_gud, 1)

            def emit_dn(e):
                for dch in range(16):
                    q = 4 * e + dch // 4
                    if dch % 4 == 0:
                        W(tensor, s_d[q % 2], 16 * (q // 2 + 1))
                    W(tensor, s_dne, 16 * e + dch - 1)
                    for th in range(2):
                        for j in range(6):
                            if dch == 0 and th == 0:
                                W(tensor, s_gue, 13 * e + 8 + j)
                            ins = tensor.matmul(pd[dch % 2][:, th * 512:(th + 1) * 512],
                                                dn_b[q % 2][:, j, (dch % 4) * 128:(dch % 4 + 1) * 128],
                                                h_b[e % 2][:, j, th * 512:(th + 1) * 512],
                                                start=(j == 0), stop=(j == 5))
                    ins.then_inc(s_dnd, 1)
                    if dch == 3 and e >= 1 and e + 1 < ne:
                        emit_ce(e + 1)

            if ne > 0:
                emit_ce(0)
                emit_gu_ocs(0, 0, 12)
            if ne > 1:
                emit_ce(1)
                emit_gu_ocs(1, 0, 2)
            tensor.wait_ge(s_gldg, 16 * 7)
            tensor.wait_ge(s_v1, V["cneg"])
            tensor.wait_ge(s_s1, S["hc3"])
            hmap = [0, 2, 1, 3]
            for dch in range(DC):
                W(tensor, s_yev, dch - 1)
                for th in range(2):
                    for bi in range(4):
                        ins = tensor.matmul(pd[dch % 2][:, th * 512:(th + 1) * 512],
                                            ctbuf[:, bi, dch * 128:(dch + 1) * 128],
                                            H_bf[:, hmap[bi], th * 512:(th + 1) * 512],
                                            start=(bi == 0), stop=(bi == 3))
                ins.then_inc(s_t1, 1)
            if ne > 0:
                tensor.wait_ge(s_yev, 16)
                emit_dn(0)
                for e in range(1, ne):
                    emit_gu_ocs(e, 2 if e == 1 else 0, 12)
                    emit_dn(e)


        @blk.scalar
        def _(scalar):
            def ds():
                scalar.drain()
            scalar.wait_ge(s_misc, 16 * 9)
            scalar.activation(aparm[:, 0:2], nu2[:], ACT.Exp)
            ds()
            scalar.activation(aparm[:, 2:4], aparm[:, 0:2], ACT.Exp, scale=-1.0, bias=-1.0)
            scalar.activation(aparm[:, 4:6], th2[:], ACT.Exp)
            scalar.drain().then_inc(s_s1, 1)
            for i in range(8):
                W(scalar, s_tok[i % 2], 16 * (i // 2 + 1))
                if i >= 2:
                    W(scalar, s_v1, V[f"stat{i - 2}"])
                scalar.activation(sq2[i % 2], tokstage[i % 2], ACT.Square).then_inc(s_s1, 1)
            scalar.wait_ge(s_v1, V["stat0"])
            scalar.activation(aparm[:, 20:22], aparm[:, 18:20], ACT.Ln)
            ds()
            scalar.activation(aparm[:, 16:18], aparm[:, 20:22], ACT.Exp, scale=0.5)
            scalar.drain().then_inc(s_s1, 1)
            scalar.wait_ge(s_v1, V["var"])
            scalar.activation(mq[:, 32:40], mq[:, 32:40], ACT.Ln, bias=consts[:, 1:2])
            ds()
            scalar.activation(mq[:, 24:32], mq[:, 32:40], ACT.Exp, scale=-0.5)
            scalar.drain().then_inc(s_s1, 1)
            for k in range(16):
                scalar.wait_ge(s_v1, V[f"xn{k}"])
                scalar.activation(xn_bf[:, k, :], xn32[k % 2], ACT.Copy).then_inc(s_s1, 1)
            scalar.wait_ge(s_v1, V["ltok"])
            for tc in range(8):
                scalar.activation(gat2[:, tc, :], gat[:, tc, :], ACT.Exp,
                                  bias=gsml[:, 8 + tc:9 + tc])
            scalar.drain().then_inc(s_s1, 1)
            scalar.wait_ge(s_v1, V["ssum8"])
            scalar.activation(gsml[:, 24:32], gsml[:, 16:24], ACT.Ln)
            ds()
            scalar.activation(gsml[:, 24:32], gsml[:, 24:32], ACT.Exp, scale=-1.0)
            scalar.drain().then_inc(s_s1, 1)
            scalar.wait_ge(s_v1, V["den"])
            scalar.activation(gsml[:, 40:48], gsml[:, 40:48], ACT.Ln)
            ds()
            scalar.activation(gsml[:, 40:48], gsml[:, 40:48], ACT.Exp, scale=-1.0)
            scalar.drain().then_inc(s_s1, 1)
            scalar.activation(aparm[:, 8:10], aparm[:, 6:8], ACT.Sin)
            scalar.activation(aparm[:, 10:12], aparm[:, 22:24], ACT.Sin)
            scalar.drain().then_inc(s_s1, 1)
            for j in range(6 if ne > 0 else 0):
                scalar.wait_ge(s_gud, 2 + j)
                scalar.activation(gsil[:, j, :], pg[j % 2][:, :], ACT.Silu).then_inc(s_sil, 1)
            scalar.wait_ge(s_v1, V["hfin"])
            scalar.activation(H_bf[:, 0, :], png_re[0][:, :], ACT.Copy).then_inc(s_s1, 1)
            scalar.activation(H_bf[:, 1, :], png_im[0][:, :], ACT.Copy).then_inc(s_s1, 1)
            scalar.activation(H_bf[:, 2, :], png_re[1][:, :], ACT.Copy).then_inc(s_s1, 1)
            scalar.activation(H_bf[:, 3, :], png_im[1][:, :], ACT.Copy)
            scalar.drain().then_inc(s_s1, 1)
            for e in range(1, ne):
                for j in range(6):
                    scalar.wait_ge(s_gud, 13 * e + 2 + j)
                    scalar.activation(gsil[:, j, :], pg[j % 2][:, :], ACT.Silu).then_inc(s_sil, 1)

        @blk.vector
        def _(vector):
            def dv():
                vector.drain()
            vector.wait_ge(s_s1, S["aexp"])
            vector.tensor_scalar(aparm[:, 6:8], aparm[:, 4:6], 1.0 / TWO_PI, MAGIC, ALU.mult, ALU.add)
            dv()
            vector.tensor_scalar(aparm[:, 6:8], aparm[:, 6:8], -MAGIC, None, ALU.add)
            dv()
            vector.scalar_tensor_tensor(aparm[:, 6:8], aparm[:, 6:8], -TWO_PI, aparm[:, 4:6],
                                        ALU.mult, ALU.add)
            dv()
            vector.tensor_scalar(aparm[:, 22:24], aparm[:, 6:8], TWO_PI / 4, None, ALU.add)
            dv()
            vector.tensor_scalar(aparm[:, 20:22], aparm[:, 22:24], TWO_PI / 2, None, ALU.is_gt)
            dv()
            vector.scalar_tensor_tensor(aparm[:, 22:24], aparm[:, 20:22], -TWO_PI, aparm[:, 22:24],
                                        ALU.mult, ALU.add)
            dv()
            vector.tensor_tensor(aparm[:, 18:20], aparm[:, 2:4], aparm[:, 2:4], ALU.mult)
            dv()
            vector.tensor_scalar(aparm[:, 18:20], aparm[:, 18:20], -1.0, 1.0, ALU.mult, ALU.add)
            dv()
            for i in range(8):
                W(vector, s_tok[i % 2], 16 * (i // 2 + 1))
                W(vector, s_s1, S[f"sq{i}"])
                vector.reduce_sum(mq[:, 8 + i:9 + i], sq2[i % 2], AX.X)
                vector.reduce_sum(mq[:, i:i + 1], tokstage[i % 2], AX.X).then_inc(s_v1, 1)
            dv()
            vector.tensor_scalar(mq[:, 16:24], mq[:, 0:8], 1.0 / D, None, ALU.mult)
            vector.tensor_scalar(mq[:, 32:40], mq[:, 8:16], 1.0 / D, None, ALU.mult)
            dv()
            vector.tensor_tensor(mq[:, 0:8], mq[:, 16:24], mq[:, 16:24], ALU.mult)
            dv()
            vector.tensor_tensor(mq[:, 32:40], mq[:, 32:40], mq[:, 0:8], ALU.subtract)
            vector.drain().then_inc(s_v1, 1)
            vector.tensor_copy(mr16[:, 0:8], mq[:, 16:24])
            vector.wait_ge(s_s1, S["r8"])
            vector.tensor_copy(mr16[:, 8:16], mq[:, 24:32])
            vector.drain().then_inc(s_v1, 1)
            vector.wait_ge(s_t1, T1["mrT"])
            vector.tensor_copy(mr_sb[:], pA[0:16, 0:128])
            vector.drain().then_inc(s_v1, 1)
            vector.wait_ge(s_t1, T1["rbc"])
            vector.tensor_copy(m_b[:, :], pA[:])
            vector.tensor_copy(r_b[:, :], pB[:])
            vector.drain().then_inc(s_v1, 1)
            for k in range(16):
                W(vector, s_xc[k % 2], 16 * (k // 2 + 1))
                if k >= 2:
                    vector.wait_ge(s_s1, S[f"cast{k - 2}"])
                vector.tensor_tensor(xn32[k % 2], xstage[k % 2], m_b[:, :], ALU.subtract)
                vector.tensor_tensor(xn32[k % 2], xn32[k % 2], r_b[:, :], ALU.mult)
                vector.scalar_tensor_tensor(out_acc[:, k, :], xn32[k % 2], d_col[:, k:k + 1],
                                            xstage[k % 2], ALU.mult, ALU.add)
                vector.drain().then_inc(s_v1, 1)
            vector.wait_ge(s_t1, T1["lg"])
            vector.tensor_copy(lgT_sb[:], pC[0:32, :])
            vector.drain().then_inc(s_v1, 1)
            vector.wait_ge(s_t1, T1["bu_re0"])
            vector.tensor_scalar(bufA_re[:, :], pg[0][:], aparm[:, 16 + 0:17 + 0],
                                 None, ALU.mult).then_inc(s_v1, 1)
            vector.wait_ge(s_t1, T1["bu_re1"])
            vector.tensor_scalar(bufB_re[:, :], pg[1][:], aparm[:, 16 + 1:17 + 1],
                                 None, ALU.mult).then_inc(s_v1, 1)
            vector.wait_ge(s_t1, T1["gt7"])
            for tc in range(8):
                vector.tensor_copy(gat[:, tc, :], pD[0:128, tc * 128:tc * 128 + 32])
            dv()
            vector.reduce_max(gsml[:, 0:8], gat[:], AX.X)
            dv()
            vector.tensor_scalar(gsml[:, 8:16], gsml[:, 0:8], -1.0, None, ALU.mult)
            vector.drain().then_inc(s_v1, 1)
            vector.wait_ge(s_s1, S["gexp"])
            vector.reduce_sum(gsml[:, 16:24], gat2[:], AX.X)
            vector.drain().then_inc(s_v1, 1)
            vector.wait_ge(s_s1, S["inv"])
            for tc in range(8):
                vector.tensor_scalar(gat2[:, tc, :], gat2[:, tc, :], gsml[:, 24 + tc:25 + tc],
                                     None, ALU.mult)
            dv()
            for tc in range(8):
                vector.max(tv[:, tc, :], gat2[:, tc, :])
            dv()
            vector.reduce_sum(gsml[:, 16:24], tv[:], AX.X)
            vector.tensor_copy(cumb[:], tv[:])
            dv()
            for j in range(1, 8):
                vector.tensor_tensor(cumb[:, :, j:j + 1], cumb[:, :, j - 1:j],
                                     tv[:, :, j:j + 1], ALU.add)
                dv()
            vector.tensor_scalar(gsml[:, 32:40], gsml[:, 16:24], 0.8, None, ALU.mult)
            dv()
            for tc in range(8):
                vector.tensor_scalar(cumb[:, tc, :], cumb[:, tc, :], gsml[:, 32 + tc:33 + tc],
                                     None, ALU.is_lt)
            dv()
            for tc in range(8):
                vector.tensor_copy(gat3[:, tc, 0:7], cumb[:, tc, 0:7])
                vector.tensor_scalar(gat3[:, tc, 7:8], gat2[:, tc, 0:1], 0.0, None, ALU.mult)
            dv()
            for tc in range(8):
                vector.tensor_copy(cumb[:, tc, 1:8], gat3[:, tc, 0:7])
                vector.tensor_scalar(cumb[:, tc, 0:1], gat2[:, tc, 0:1], 0.0, 1.0, ALU.mult, ALU.add)
            dv()
            vector.tensor_scalar(cumb[:].rearrange("p a b -> p (a b)"),
                                 cumb[:].rearrange("p a b -> p (a b)"),
                                 -3.0e38, 3.0e38, ALU.mult, ALU.add)
            dv()
            vector.tensor_tensor(cumb[:], cumb[:], tv[:], ALU.add)
            dv()
            vector.tensor_reduce(gsml[:, 32:40], cumb[:], AX.X, ALU.min)
            dv()
            for tc in range(8):
                vector.tensor_scalar(gat3[:, tc, :], gat2[:, tc, :], gsml[:, 32 + tc:33 + tc],
                                     None, ALU.is_ge)
            dv()
            vector.tensor_tensor(gat3[:], gat2[:], gat3[:], ALU.mult)
            dv()
            vector.reduce_sum(gsml[:, 40:48], gat3[:], AX.X)
            vector.drain().then_inc(s_v1, 1)
            vector.wait_ge(s_s1, S["dinv"])
            for tc in range(8):
                vector.tensor_scalar(gat2[:, tc, :], gat3[:, tc, :],
                                     gsml[:, 40 + tc:41 + tc], None, ALU.mult)
            vector.drain().then_inc(s_v1, 1)
            vector.wait_ge(s_t1, T1["ct7"])
            for tc in range(8):
                vector.tensor_copy(combT[:, tc * 128:(tc + 1) * 128],
                                   pD[0:32, tc * 128:(tc + 1) * 128])
            vector.drain().then_inc(s_v1, 1)
            vector.wait_ge(s_t1, T1["bu_im0"])
            vector.tensor_scalar(bufA_im[:, :], pg[0][:], aparm[:, 16 + 0:17 + 0],
                                 None, ALU.mult).then_inc(s_v1, 1)
            vector.wait_ge(s_t1, T1["bu_im1"])
            vector.tensor_scalar(bufB_im[:, :], pg[1][:], aparm[:, 16 + 1:17 + 1],
                                 None, ALU.mult).then_inc(s_v1, 1)
            vector.wait_ge(s_s1, S["sins"])
            vector.tensor_tensor(aparm[:, 12:14], aparm[:, 2:4], aparm[:, 10:12], ALU.mult)
            vector.tensor_tensor(aparm[:, 14:16], aparm[:, 2:4], aparm[:, 8:10], ALU.mult)
            dv()
            vector.tensor_copy(apow_re[:, 0, :], aparm[:, 12:14])
            vector.tensor_copy(apow_im[:, 0, :], aparm[:, 14:16])
            dv()
            for k in range(10):
                re_k = apow_re[:, k, :]; im_k = apow_im[:, k, :]
                vector.tensor_tensor(aparm[:, 20:22], re_k, re_k, ALU.mult)
                vector.tensor_tensor(aparm[:, 22:24], im_k, im_k, ALU.mult)
                dv()
                vector.tensor_tensor(apow_re[:, k + 1, :], aparm[:, 20:22], aparm[:, 22:24],
                                     ALU.subtract)
                vector.tensor_tensor(aparm[:, 20:22], re_k, im_k, ALU.mult)
                dv()
                vector.tensor_scalar(apow_im[:, k + 1, :], aparm[:, 20:22], 2.0, None, ALU.mult)
                dv()
            vector.tensor_scalar(napow_im[:].rearrange("p a b -> p (a b)"),
                                 apow_im[:].rearrange("p a b -> p (a b)"), -1.0, None, ALU.mult)
            vector.tensor_scalar(qp_re[:, 0, :], apow_re[:, 0, :], 0.0, 1.0, ALU.mult, ALU.add)
            vector.tensor_scalar(qp_im[:, 0, :], apow_re[:, 0, :], 0.0, None, ALU.mult)
            dv()
            for p in range(7):
                a10r = apow_re[:, 10, :]; a10i = apow_im[:, 10, :]
                vector.tensor_tensor(aparm[:, 20:22], qp_re[:, p, :], a10r, ALU.mult)
                vector.tensor_tensor(aparm[:, 22:24], qp_im[:, p, :], a10i, ALU.mult)
                dv()
                vector.tensor_tensor(qp_re[:, p + 1, :], aparm[:, 20:22], aparm[:, 22:24],
                                     ALU.subtract)
                vector.tensor_tensor(aparm[:, 20:22], qp_re[:, p, :], a10i, ALU.mult)
                dv()
                vector.tensor_tensor(aparm[:, 22:24], qp_im[:, p, :], a10r, ALU.mult)
                dv()
                vector.tensor_tensor(qp_im[:, p + 1, :], aparm[:, 20:22], aparm[:, 22:24],
                                     ALU.add)
                dv()
            vector.tensor_copy(qp_re[0:1, 0, 0:1], qp_re[0:1, 0, 0:1])
            vector.drain().then_inc(s_v1, 1)
            mcur_re = [png_re[0][:, 0:128], png_re[1][:, 0:128]]
            mcur_im = [png_im[0][:, 0:128], png_im[1][:, 0:128]]
            mnxt_re = [png_re[0][:, 128:256], png_re[1][:, 128:256]]
            mnxt_im = [png_im[0][:, 128:256], png_im[1][:, 128:256]]
            for ch in range(2):
                a_re = [bufA_re, bufB_re][ch]; a_im = [bufA_im, bufB_im][ch]
                sre = apow_re[:, 0, ch:ch + 1]
                sim_ = apow_im[:, 0, ch:ch + 1]
                nsim = napow_im[:, 0, ch:ch + 1]
                s_r = a_re[:, 896:T]; s_i = a_im[:, 896:T]
                b_re = mcur_re[ch]; b_im = mcur_im[ch]
                vector.scalar_tensor_tensor(b_re[:, 1:128], s_r[:, 0:127], sre,
                                            s_r[:, 1:128], ALU.mult, ALU.add)
                vector.scalar_tensor_tensor(b_re[:, 1:128], s_i[:, 0:127], nsim,
                                            b_re[:, 1:128], ALU.mult, ALU.add)
                vector.scalar_tensor_tensor(b_im[:, 1:128], s_i[:, 0:127], sre,
                                            s_i[:, 1:128], ALU.mult, ALU.add)
                vector.scalar_tensor_tensor(b_im[:, 1:128], s_r[:, 0:127], sim_,
                                            b_im[:, 1:128], ALU.mult, ALU.add)
                vector.tensor_copy(b_re[:, 0:1], s_r[:, 0:1])
                vector.tensor_copy(b_im[:, 0:1], s_i[:, 0:1])
                dv()
            for k in range(1, 7):
                dstep = 1 << k
                n = 128 - dstep
                for ch in range(2):
                    sre = apow_re[:, k, ch:ch + 1]
                    sim_ = apow_im[:, k, ch:ch + 1]
                    nsim = napow_im[:, k, ch:ch + 1]
                    a_re, a_im = mcur_re[ch], mcur_im[ch]
                    b_re, b_im = mnxt_re[ch], mnxt_im[ch]
                    vector.scalar_tensor_tensor(b_re[:, dstep:128], a_re[:, 0:n], sre,
                                                a_re[:, dstep:128], ALU.mult, ALU.add)
                    vector.scalar_tensor_tensor(b_re[:, dstep:128], a_im[:, 0:n], nsim,
                                                b_re[:, dstep:128], ALU.mult, ALU.add)
                    vector.scalar_tensor_tensor(b_im[:, dstep:128], a_im[:, 0:n], sre,
                                                a_im[:, dstep:128], ALU.mult, ALU.add)
                    vector.scalar_tensor_tensor(b_im[:, dstep:128], a_re[:, 0:n], sim_,
                                                b_im[:, dstep:128], ALU.mult, ALU.add)
                    vector.tensor_copy(b_re[:, 0:dstep], a_re[:, 0:dstep])
                    vector.tensor_copy(b_im[:, 0:dstep], a_im[:, 0:dstep])
                    dv()
                mcur_re, mnxt_re = mnxt_re, mcur_re
                mcur_im, mnxt_im = mnxt_im, mcur_im
            for ch in range(2):
                vector.tensor_copy(f_sb[:, 2 * ch:2 * ch + 1], mcur_re[ch][:, 127:128])
                vector.tensor_copy(f_sb[:, 2 * ch + 1:2 * ch + 2], mcur_im[ch][:, 127:128])
            vector.drain().then_inc(s_v1, 1)
            def e0_evacs(k):
                if ne == 0:
                    return
                sched = {0: [("ce", 0)], 1: [("g", 0)], 2: [("g", 1), ("g", 2)],
                         3: [("g", 3), ("g", 4)], 4: [("g", 5)]}
                for kind, j in sched.get(k, []):
                    if kind == "ce":
                        vector.wait_ge(s_gud, 1)
                        vector.tensor_copy(ce_b[0][:, :], pA[:]).then_inc(s_gue, 1)
                    elif kind == "g":
                        vector.wait_ge(s_sil, j + 1)
                        vector.tensor_tensor(gsil[:, j, :], gsil[:, j, :], ce_b[0][:, :],
                                             ALU.mult).then_inc(s_gue, 1)
                    else:
                        vector.wait_ge(s_gud, 8 + j)
                        vector.tensor_tensor(h_b[0][:, j, :], gsil[:, j, :], pg[j % 2][:],
                                             ALU.mult).then_inc(s_gue, 1)

            vector.wait_ge(s_t1, T1["bu_im1"])
            cur_re = [bufA_re, bufB_re]; cur_im = [bufA_im, bufB_im]
            nxt_re = [png_re[0], png_re[1]]; nxt_im = [png_im[0], png_im[1]]
            for k in range(7):
                e0_evacs(k)
                dstep = 1 << k
                n = T - dstep
                for ch in range(2):
                    sre = apow_re[:, k, ch:ch + 1]
                    sim_ = apow_im[:, k, ch:ch + 1]
                    nsim = napow_im[:, k, ch:ch + 1]
                    a_re, a_im = cur_re[ch], cur_im[ch]
                    b_re, b_im = nxt_re[ch], nxt_im[ch]
                    vector.scalar_tensor_tensor(b_re[:, dstep:T], a_re[:, 0:n], sre,
                                                a_re[:, dstep:T], ALU.mult, ALU.add)
                    vector.scalar_tensor_tensor(b_re[:, dstep:T], a_im[:, 0:n], nsim,
                                                b_re[:, dstep:T], ALU.mult, ALU.add)
                    vector.scalar_tensor_tensor(b_im[:, dstep:T], a_im[:, 0:n], sre,
                                                a_im[:, dstep:T], ALU.mult, ALU.add)
                    vector.scalar_tensor_tensor(b_im[:, dstep:T], a_re[:, 0:n], sim_,
                                                b_im[:, dstep:T], ALU.mult, ALU.add)
                    vector.tensor_copy(b_re[:, 0:dstep], a_re[:, 0:dstep])
                    vector.tensor_copy(b_im[:, 0:dstep], a_im[:, 0:dstep])
                    dv()
                cur_re, nxt_re = nxt_re, cur_re
                cur_im, nxt_im = nxt_im, cur_im
            vector.wait_ge(s_fal, 16)
            for p in range(7):
                for j in range(7):
                    vector.tensor_scalar(gtmp[:, p, :, j], f_all[:, j, :],
                                         selc[:, p * 8 + j:p * 8 + j + 1], None, ALU.mult)
            dv()
            vector.tensor_reduce(gacc[:, 0:7, :], gtmp[:], AX.X, ALU.add)
            vector.tensor_scalar(gacc[:, 7, :], f_all[:, 7, :], 0.0, None, ALU.mult)
            dv()
            for p in range(8):
                for ch in range(2):
                    gre = gacc[:, p, 2 * ch:2 * ch + 1]
                    gim = gacc[:, p, 2 * ch + 1:2 * ch + 2]
                    qre = qp_re[:, p, ch:ch + 1]
                    qim = qp_im[:, p, ch:ch + 1]
                    vector.tensor_scalar(qg[:, p, ch, 0:1], gre, qre, None, ALU.mult)
                    vector.tensor_scalar(qg[:, p, ch, 1:2], gim, qim, None, ALU.mult)
                    vector.tensor_scalar(qg[:, p, ch, 2:3], gim, qre, None, ALU.mult)
                    vector.tensor_scalar(qg[:, p, ch, 3:4], gre, qim, None, ALU.mult)
            dv()
            for ch in range(2):
                for t4 in range(4):
                    vector.tensor_reduce(hsum[:, ch, t4:t4 + 1],
                                         qg[:, :, ch, t4:t4 + 1].rearrange("p a b -> p (a b)"),
                                         AX.X, ALU.add)
            dv()
            for ch in range(2):
                vector.tensor_tensor(hin[:, 2 * ch:2 * ch + 1], hsum[:, ch, 0:1],
                                     hsum[:, ch, 1:2], ALU.subtract)
                vector.tensor_tensor(hin[:, 2 * ch + 1:2 * ch + 2], hsum[:, ch, 2:3],
                                     hsum[:, ch, 3:4], ALU.add)
            dv()
            for j in range(3 if ne > 0 else 0):
                vector.wait_ge(s_gud, 8 + j)
                vector.tensor_tensor(h_b[0][:, j, :], gsil[:, j, :], pg[j % 2][:],
                                     ALU.mult).then_inc(s_gue, 1)
            corr_re = [bufA_re, bufB_re]; corr_im = [bufA_im, bufB_im]
            for ch in range(2):
                c_re, c_im = corr_re[ch], corr_im[ch]
                hre = hin[:, 2 * ch:2 * ch + 1]; him = hin[:, 2 * ch + 1:2 * ch + 2]
                are = apow_re[:, 0, ch:ch + 1]; aim = apow_im[:, 0, ch:ch + 1]
                vector.tensor_scalar(c_re[:, 0:1], hre, are, None, ALU.mult)
                vector.tensor_scalar(aparm[:, 20:21], him, aim, None, ALU.mult)
                dv()
                vector.tensor_tensor(c_re[:, 0:1], c_re[:, 0:1], aparm[:, 20:21], ALU.subtract)
                vector.tensor_scalar(c_im[:, 0:1], him, are, None, ALU.mult)
                vector.tensor_scalar(aparm[:, 20:21], hre, aim, None, ALU.mult)
                dv()
                vector.tensor_tensor(c_im[:, 0:1], c_im[:, 0:1], aparm[:, 20:21], ALU.add)
                dv()
                for k in range(7):
                    dstep = 1 << k
                    hi = min(2 * dstep, T)
                    n = hi - dstep
                    sre = apow_re[:, k, ch:ch + 1]
                    sim_ = apow_im[:, k, ch:ch + 1]
                    nsim = napow_im[:, k, ch:ch + 1]
                    vector.tensor_scalar(c_re[:, dstep:hi], c_re[:, 0:n], sre, None, ALU.mult)
                    dv()
                    vector.scalar_tensor_tensor(c_re[:, dstep:hi], c_im[:, 0:n], nsim,
                                                c_re[:, dstep:hi], ALU.mult, ALU.add)
                    vector.tensor_scalar(c_im[:, dstep:hi], c_im[:, 0:n], sre, None, ALU.mult)
                    dv()
                    vector.scalar_tensor_tensor(c_im[:, dstep:hi], c_re[:, 0:n], sim_,
                                                c_im[:, dstep:hi], ALU.mult, ALU.add)
                    dv()
                vector.tensor_tensor(cur_re[ch][:, 0:128], cur_re[ch][:, 0:128],
                                     c_re[:, 0:128], ALU.add)
                vector.tensor_tensor(cur_im[ch][:, 0:128], cur_im[ch][:, 0:128],
                                     c_im[:, 0:128], ALU.add)
            vector.drain().then_inc(s_v1, 1)
            vector.wait_ge(s_gldg, 16 * 7)
            vector.tensor_scalar(ctbuf[:, 2, :], ctbuf[:, 2, :], -1.0, None, ALU.mult)
            vector.tensor_scalar(ctbuf[:, 3, :], ctbuf[:, 3, :], -1.0, None, ALU.mult)
            vector.drain().then_inc(s_v1, 1)
            for j in range(3, 6) if ne > 0 else []:
                vector.wait_ge(s_gud, 8 + j)
                vector.tensor_tensor(h_b[0][:, j, :], gsil[:, j, :], pg[j % 2][:],
                                     ALU.mult).then_inc(s_gue, 1)
            if ne > 1:
                vector.wait_ge(s_gud, 14)
                vector.tensor_copy(ce_b[1][:, :], pA[:]).then_inc(s_gue, 1)
            for dch in range(DC):
                vector.wait_ge(s_t1, T1[f"y{dch}"])
                vector.tensor_tensor(out_acc[:, dch, :], out_acc[:, dch, :],
                                     pd[dch % 2][:], ALU.add).then_inc(s_yev, 1)
            def emit_ce_evac(e):
                vector.wait_ge(s_gud, 13 * e + 1)
                vector.tensor_copy(ce_b[e % 2][:, :], pA[:]).then_inc(s_gue, 1)

            for e in range(ne):
                if e >= 1:
                    for j in range(6):
                        vector.wait_ge(s_sil, 6 * e + j + 1)
                        vector.tensor_tensor(gsil[:, j, :], gsil[:, j, :], ce_b[e % 2][:, :],
                                             ALU.mult).then_inc(s_gue, 1)
                    for j in range(6):
                        if j == 0:
                            W(vector, s_dnd, 16 * (e - 1))
                        vector.wait_ge(s_gud, 13 * e + 8 + j)
                        vector.tensor_tensor(h_b[e % 2][:, j, :], gsil[:, j, :], pg[j % 2][:],
                                             ALU.mult).then_inc(s_gue, 1)
                for dch in range(16):
                    vector.wait_ge(s_dnd, 16 * e + dch + 1)
                    vector.tensor_tensor(out_acc[:, dch, :], out_acc[:, dch, :],
                                         pd[dch % 2][:], ALU.add).then_inc(s_dne, 1)
                    if dch == 3 and e >= 1 and e + 1 < ne:
                        emit_ce_evac(e + 1)

    return nc


_NC_CACHE = None


def _get_nc():
    global _NC_CACHE
    if _NC_CACHE is None:
        _NC_CACHE = build_nc()
    return _NC_CACHE


def kernel(L0_attn_x, L0_attn_nu_log, L0_attn_theta_log, L0_attn_B_re, L0_attn_B_im,
           L0_attn_C_re, L0_attn_C_im, L0_attn_D, L0_ffn_gate_weight,
           L0_ffn_gate_up_proj, L0_ffn_down_proj):
    C = np.ascontiguousarray
    x = np.asarray(L0_attn_x, np.float32).reshape(8192, 2048)
    xT_full = C(x.T)
    nu_t = C(np.asarray(L0_attn_nu_log, np.float32).reshape(2, 128).T)
    theta_t = C(np.asarray(L0_attn_theta_log, np.float32).reshape(2, 128).T)
    d_t = C(np.asarray(L0_attn_D, np.float32).reshape(16, 128).T)
    bre_t = C(np.asarray(L0_attn_B_re, np.float32).T)
    bim_t = C(np.asarray(L0_attn_B_im, np.float32).T)
    cre_t = C(np.asarray(L0_attn_C_re, np.float32).T)
    cim_t = C(np.asarray(L0_attn_C_im, np.float32).T)
    gw_t = C(np.asarray(L0_ffn_gate_weight, np.float32).T)
    gup = np.asarray(L0_ffn_gate_up_proj, np.float32)
    gu_w = C(gup.transpose(0, 2, 1).reshape(NE, D, 6, 256).transpose(0, 2, 1, 3))
    dnp = np.asarray(L0_ffn_down_proj, np.float32)
    dn_w = C(dnp.transpose(0, 2, 1).reshape(NE, 768, 4, 512).transpose(0, 2, 1, 3))
    idm_bf = np.eye(128, dtype=ml_dtypes.bfloat16)
    idm_f32 = np.eye(128, dtype=np.float32)
    ones_bf = np.ones((1, 128), dtype=ml_dtypes.bfloat16)
    ones_f32_np = np.ones((1, 128), dtype=np.float32)
    consts_np = np.tile(np.array([[-1.0, 1e-5]], np.float32), (128, 1))

    in_maps = []
    for c in range(NCORES):
        selm = np.zeros((8, 8), np.float32)
        for j in range(c):
            selm[c - 1 - j, j] = 1.0
        selc_np = np.broadcast_to(selm.reshape(1, 64), (128, 64)).copy()
        in_maps.append(dict(
            xT=C(xT_full[:, c * T:(c + 1) * T]),
            xtok=C(x[c * T:(c + 1) * T, :]),
            nu_t=nu_t, theta_t=theta_t, d_t=d_t, selc_t=selc_np,
            bre_t=bre_t, bim_t=bim_t, cre_t=cre_t, cim_t=cim_t,
            gw_t=gw_t, gu_w=gu_w, dn_w=dn_w,
            idm_bf=idm_bf, idm_f32=idm_f32, ones_bf=ones_bf,
            ones_f32=ones_f32_np, consts_d=consts_np,
        ))
    nc = _get_nc()
    res = run_bass_kernel_spmd(nc, in_maps, core_ids=list(range(NCORES)))
    outT = np.concatenate([np.asarray(r["oT"]) for r in res.results], axis=1)
    return C(outT.T).reshape(1, 8192, 2048)
